# version 29
# baseline (speedup 1.0000x reference)
import numpy as np
import ml_dtypes
import concourse.bass as bass
import concourse.mybir as mybir
from concourse.bass_utils import run_bass_kernel_spmd
from contextlib import ExitStack

F32 = mybir.dt.float32
BF16 = mybir.dt.bfloat16
AF = mybir.ActivationFunctionType
ALU = mybir.AluOpType
AX = mybir.AxisListType
BF = ml_dtypes.bfloat16

C = 1024
CB = 8
FF = 4096
FB = 32
R = 16
TC = 512
NB = 2
NCH = 16
NCORE = 8
ROWS = NB * 4096

NQS = 12
NQA = 6


class Sch:

    ENGS = ("tensor", "vector", "scalar", "sync", "gpsimd")

    def __init__(self):
        self.ops = {e: [] for e in self.ENGS}
        self.cnt = {"pe": 0, "v": 0, "s": 0, "g": 0}
        for i in range(NQS):
            self.cnt[f"qs{i}"] = 0
        for i in range(NQA):
            self.cnt[f"qa{i}"] = 0
        self.rr = {"qs": 0, "qa": 0}
        self.guar = {e: {} for e in self.ENGS}
        self.last = {"vector": None, "scalar": None, "gpsimd": None}

    def emit(self, eng, fn, deps=(), sem=None, step=1):
        deps = [d for d in deps if d is not None]
        if eng in self.last and self.last[eng] is not None:
            deps.append(self.last[eng])
        waits = []
        g = self.guar[eng]
        for k, v in deps:
            if v > g.get(k, 0):
                waits.append((k, v))
                g[k] = v
        tok = None
        if sem is not None:
            self.cnt[sem] += step
            tok = (sem, self.cnt[sem])
        self.ops[eng].append((waits, fn, sem, step))
        if eng in self.last and sem in ("v", "s", "g"):
            self.last[eng] = tok
        return tok

    def emit_dma(self, eng, fn, deps, pool):
        i = self.rr[pool] % (NQS if pool == "qs" else NQA)
        self.rr[pool] += 1
        sem = f"{pool}{i}"
        deps = list(deps) + [(sem, self.cnt[sem])] if self.cnt[sem] else list(deps)
        return self.emit(eng, fn, deps, sem, 16)


def build_nc():
    nc = bass.Bass("TRN2", target_bir_lowering=False, debug=False,
                   dynamic_dma_scratch_size=2048)

    def din(name, shape, dt=F32):
        return nc.dram_tensor(name, shape, dt, kind="ExternalInput").ap()

    xd = din("xd", [ROWS, C])
    y = nc.dram_tensor("y", [ROWS, C], F32, kind="ExternalOutput").ap()
    w_td = din("w_td", [8, 128]); w_tf = din("w_tf", [8, 128])
    w_tmk = din("w_tmk", [8, 128]); w_tmv = din("w_tmv", [8, 128])
    w_tmr = din("w_tmr", [8, 128])
    w_wk = din("w_wk", [C, C]); w_wv = din("w_wv", [C, C])
    w_wr = din("w_wr", [C, C]); w_wo = din("w_wo", [C, C])
    w_wfc = din("w_wfc", [FF, C]); w_bfc = din("w_bfc", [32, 128])
    w_afc = din("w_afc", [R, C]); w_bfcB = din("w_bfcB", [FF, R])
    w_wpj = din("w_wpj", [C, FF]); w_bpj = din("w_bpj", [8, 128])
    w_apj = din("w_apj", [R, FF]); w_bpjB = din("w_bpjB", [C, R])
    eye_f = din("eye_f", [128, 128]); eye_b = din("eye_b", [128, 128], BF16)
    ones1 = din("ones1", [1, 128])

    wk_s = nc.dram_tensor("wk_s", [8, 128, 1024], BF16, kind="Internal").ap()
    wv_s = nc.dram_tensor("wv_s", [8, 128, 1024], BF16, kind="Internal").ap()
    wo_s = nc.dram_tensor("wo_s", [8, 128, 1024], BF16, kind="Internal").ap()
    wr_s = nc.dram_tensor("wr_s", [8, 128, 1024], BF16, kind="Internal").ap()
    wfc_s = nc.dram_tensor("wfc_s", [FB, 128, 1024], BF16, kind="Internal").ap()
    wpj_s = nc.dram_tensor("wpj_s", [8, 128, FB * 128], BF16, kind="Internal").ap()

    ctx = ExitStack()
    with ctx:
        def sb(name, shape, dt=F32):
            return ctx.enter_context(nc.sbuf_tensor(name, shape, dt))

        a_fc_t = sb("a_fc_t", [128, 128], BF16)
        b_fc_t = sb("b_fc_t", [16, 4096], BF16)
        a_pj_t = sb("a_pj_t", [128, 512], BF16)
        b_pj_t = sb("b_pj_t", [16, 1024], BF16)
        vD = sb("vD", [128, 8]); vU = sb("vU", [128, 8])
        vmk = sb("vmk", [128, 8]); vmv = sb("vmv", [128, 8]); vmr = sb("vmr", [128, 8])
        bfc_c = sb("bfc_c", [128, 32]); bpj_c = sb("bpj_c", [128, 8])
        scl = sb("scl", [128, 8])
        s_eyef = sb("s_eyef", [128, 128]); s_eyeb = sb("s_eyeb", [128, 128], BF16)
        s_ones = sb("s_ones", [1, 128])
        wk_buf = sb("wk_buf", [128, 3072], BF16)
        wv_buf = sb("wv_buf", [128, 3072], BF16)
        wo_buf = sb("wo_buf", [128, 3072], BF16)
        wr_buf = sb("wr_buf", [128, 3072], BF16)
        wfc_buf = sb("wfc_buf", [128, 3072], BF16)
        wpj_buf = sb("wpj_buf", [128, 6144], BF16)
        xld = sb("xld", [128, 2048])
        xres = sb("xres", [128, 2048])
        xnn = sb("xnn", [128, 4096], BF16)
        XS = sb("XS", [128, 4096], BF16)
        xn_ct = sb("xn_ct", [128, 4096], BF16)
        mixA = sb("mixA", [128, 4096], BF16)
        mixB = sb("mixB", [128, 4096], BF16)
        EH = sb("EH", [128, 8192])
        s_sb = sb("s_sb", [128, 4096])
        PQ = sb("PQ", [128, 2048])
        SM = sb("SM", [128, 96])
        psc = sb("psc", [128, 1024])
        g_s = sb("g_s", [128, 1024])
        mid_bf = sb("mid_bf", [16, 512], BF16)
        mid2_bf = sb("mid2_bf", [16, 512], BF16)
        h_sb = sb("h_sb", [128, 16384], BF16)
        pcar = sb("pcar", [128, 8]); qcar = sb("qcar", [128, 8])
        prevc0 = sb("prevc0", [128, 8], BF16); prevc1 = sb("prevc1", [128, 8], BF16)

        P_big = ctx.enter_context(nc.psum_tensor("P_big", [128, 3584], F32))
        P_trb = ctx.enter_context(nc.psum_tensor("P_trb", [128, 1024], BF16))

        semmap = {}
        for nm in ["pe", "v", "s", "g"] + [f"qs{i}" for i in range(NQS)] + \
                  [f"qa{i}" for i in range(NQA)]:
            semmap[nm] = ctx.enter_context(nc.semaphore(f"sem_{nm}"))

        sch = Sch()

        def V(fn, deps=()):
            return sch.emit("vector", fn, deps, "v")

        def S(fn, deps=()):
            return sch.emit("scalar", fn, deps, "s")

        def T(fn, deps=(), inc=True):
            return sch.emit("tensor", fn, deps, "pe" if inc else None)

        def P(fn, deps=()):
            return sch.emit("gpsimd", fn, deps, "g")

        def DS(fn, deps=()):
            return sch.emit_dma("sync", fn, deps, "qs")

        def DA(fn, deps=()):
            return sch.emit_dma("scalar", fn, deps, "qa")

        h_bf = h_sb[:, :]

        def sb3(sbuf_ap, nblk):
            return sbuf_ap.rearrange("p (i c) -> p i c", i=nblk)

        slot_tok = [None] * 7
        slot_i = [0]

        def pslot():
            s = slot_i[0] % 7
            slot_i[0] += 1
            return s, P_big[:, s * 512:(s + 1) * 512], slot_tok[s]

        trb_tok = [None]

        ld0 = DS(lambda e: e.dma_start(out=s_eyef[:], in_=eye_f[:, :]))
        ld1 = DS(lambda e: e.dma_start(out=s_eyeb[:], in_=eye_b[:, :]))
        ld2 = DS(lambda e: e.dma_start(out=s_ones[:], in_=ones1[:, :]))

        nat = {}
        nat_specs = [
            ("td", w_td, PQ[0:8, 128:256]),
            ("tf", w_tf, PQ[0:8, 256:384]),
            ("tmk", w_tmk, PQ[0:8, 384:512]),
            ("tmv", w_tmv, PQ[0:8, 512:640]),
            ("tmr", w_tmr, PQ[0:8, 640:768]),
            ("bfc", w_bfc, PQ[0:32, 768:896]),
            ("bpj", w_bpj, PQ[0:8, 896:1024]),
            ("afc", w_afc, s_sb[0:16, 512:1536]),
            ("apj", w_apj, EH[0:16, 0:4096]),
        ]
        for name, src, dst in nat_specs:
            tok = DS(lambda e, s=src, d=dst: e.dma_start(out=d, in_=s[:, :]))
            nat[name] = (dst, tok)
        for name, src, dst in (
            ("bpjB", w_bpjB.rearrange("(db p) r -> p db r", p=128), xnn[:, 0:256]),
            ("bfcB", w_bfcB.rearrange("(fb p) r -> p fb r", p=128), s_sb[:, 0:512]),
        ):
            pass
        d3 = s_sb[:, 1536:1664].rearrange("p (a r) -> p a r", r=16)
        nat["bpjB"] = (s_sb[:, 1536:1664], DS(
            lambda e, s=w_bpjB.rearrange("(db p) r -> p db r", p=128), d=d3:
            e.dma_start(out=d, in_=s)))
        d3 = s_sb[:, 0:512].rearrange("p (a r) -> p a r", r=16)
        nat["bfcB"] = (s_sb[:, 0:512], DS(
            lambda e, s=w_bfcB.rearrange("(fb p) r -> p fb r", p=128), d=d3:
            e.dma_start(out=d, in_=s)))

        def small_tr(src_ap, n_in, dep, out_fn):
            sl, pcols, btok = pslot()
            t = T(lambda e, s=src_ap, p=pcols, n=n_in: e.transpose(
                p[:, 0:n], s, s_eyef[0:n, 0:n]), deps=[dep, ld0, btok])
            stok = out_fn(pcols, t)
            slot_tok[sl] = stok
            return stok

        dst, tok = nat["td"]
        e1 = small_tr(dst, 8, tok, lambda p, t: S(
            lambda e, p=p: e.activation(out=SM[:, 32:40], in_=p[:, 0:8], func=AF.Exp),
            deps=[t]))
        S(lambda e: e.activation(out=vD[:], in_=SM[:, 32:40], func=AF.Exp,
                                 scale=-1.0), deps=[e1])
        dst, tok = nat["tf"]
        small_tr(dst, 8, tok, lambda p, t: S(
            lambda e, p=p: e.activation(out=vU[:], in_=p[:, 0:8], func=AF.Exp),
            deps=[t]))
        for nm, dstv in (("tmk", vmk), ("tmv", vmv), ("tmr", vmr)):
            d_, tok = nat[nm]
            small_tr(d_, 8, tok, lambda p, t, dv=dstv: S(
                lambda e, p=p, dv=dv: e.copy(dv[:], p[:, 0:8]), deps=[t]))
        d_, tok = nat["bfc"]
        small_tr(d_, 32, tok, lambda p, t: S(
            lambda e, p=p: e.copy(bfc_c[:], p[:, 0:32]), deps=[t]))
        d_, tok = nat["bpj"]
        small_tr(d_, 8, tok, lambda p, t: S(
            lambda e, p=p: e.copy(bpj_c[:], p[:, 0:8]), deps=[t]))
        d_, tok = nat["afc"]
        for cb in range(CB):
            sl, pcols, btok = pslot()
            t = T(lambda e, p=pcols, cb=cb, d=d_: e.transpose(
                p[:, 0:16], d[0:16, cb * 128:(cb + 1) * 128], s_eyef[0:16, 0:16]),
                deps=[tok, ld0, btok])
            stok = S(lambda e, p=pcols, cb=cb: e.copy(
                a_fc_t[:, cb * 16:(cb + 1) * 16], p[:, 0:16]), deps=[t])
            slot_tok[sl] = stok
        d_, tok = nat["apj"]
        for fb in range(FB):
            sl, pcols, btok = pslot()
            t = T(lambda e, p=pcols, fb=fb, d=d_: e.transpose(
                p[:, 0:16], d[0:16, fb * 128:(fb + 1) * 128], s_eyef[0:16, 0:16]),
                deps=[tok, ld0, btok])
            stok = S(lambda e, p=pcols, fb=fb: e.copy(
                a_pj_t[:, fb * 16:(fb + 1) * 16], p[:, 0:16]), deps=[t])
            slot_tok[sl] = stok
        d_, tok = nat["bfcB"]
        for fb in range(FB):
            sl, pcols, btok = pslot()
            t = T(lambda e, p=pcols, fb=fb, d=d_: e.transpose(
                p[0:16, 0:128], d[:, fb * 16:(fb + 1) * 16], s_eyef[:, :]),
                deps=[tok, ld0, btok])
            stok = S(lambda e, p=pcols, fb=fb: e.copy(
                b_fc_t[:, fb * 128:(fb + 1) * 128], p[0:16, 0:128]), deps=[t])
            slot_tok[sl] = stok
        d_, tok = nat["bpjB"]
        for db in range(8):
            sl, pcols, btok = pslot()
            t = T(lambda e, p=pcols, db=db, d=d_: e.transpose(
                p[0:16, 0:128], d[:, db * 16:(db + 1) * 16], s_eyef[:, :]),
                deps=[tok, ld0, btok])
            stok = S(lambda e, p=pcols, db=db: e.copy(
                b_pj_t[:, db * 128:(db + 1) * 128], p[0:16, 0:128]), deps=[t])
            slot_tok[sl] = stok
        last_small_pe = t

        wk_wr_tok = [None] * 8
        wv_wr_tok = [None] * 8
        wo_wr_tok = [None] * 8
        wr_wr_tok = [None] * 8
        aw_prev = {"casts": [last_small_pe], "last_pe": None}

        def attn_weight(w_dram, resident, scratch=None, wr_toks=None,
                        prev_dma=None):
            ltok = DS(lambda e: e.dma_start(
                out=sb3(EH[:, 0:8192], 8),
                in_=w_dram.rearrange("(i p) c -> p i c", p=128)),
                deps=aw_prev["casts"])
            cA = V(lambda e: e.tensor_copy(mixA[:], EH[:, 0:4096]),
                   deps=[ltok, aw_prev["last_pe"]])
            cB = V(lambda e: e.tensor_copy(mixB[:], EH[:, 4096:8192]), deps=[ltok])

            def tile(i, cb):
                buf = mixA if i < 4 else mixB
                ii = i % 4
                return buf[:, ii * 1024 + cb * 128: ii * 1024 + cb * 128 + 128]

            last = None
            if True:
                buf = wo_buf
                for db in range(8):
                    for cb in range(CB):
                        last = T(lambda e, db=db, cb=cb: e.transpose(
                            P_trb[:, cb * 128:(cb + 1) * 128], tile(db, cb),
                            s_eyeb[:, :]),
                            deps=([cA, cB, ld1, trb_tok[0],
                                   wr_toks[db - 1] if db > 0 else None]
                                  if cb == 0 else []),
                            inc=(cb == 7))
                    stok = S(lambda e: e.copy(buf[:, 0:1024], P_trb[:, :]),
                             deps=[last, prev_dma if db == 0 else None])
                    trb_tok[0] = stok
                    wr_toks[db] = DS(lambda e, db=db, sc=scratch: e.dma_start(
                        out=sc[db, :, :], in_=buf[:, 0:1024]), deps=[stok])
            aw_prev["casts"] = [cA, cB]
            aw_prev["last_pe"] = last

        attn_weight(w_wk, None, wk_s, wk_wr_tok)
        attn_weight(w_wv, None, wv_s, wv_wr_tok, prev_dma=wk_wr_tok[7])
        attn_weight(w_wr, None, wr_s, wr_wr_tok, prev_dma=wv_wr_tok[7])
        attn_weight(w_wo, None, wo_s, wo_wr_tok, prev_dma=wr_wr_tok[7])

        def absmax_of(w_dram, npieces, piece_rows, acc_col, first_deps=()):
            prev_half = [None, None]
            prev = None
            pr2 = piece_rows // 2
            for p_ in range(npieces * 2):
                src = w_dram[p_ * pr2:(p_ + 1) * pr2, :]
                nb = pr2 // 128
                half = p_ % 2
                dep = ([prev_half[half]] if prev_half[half]
                       else list(first_deps))
                ltok = DS(lambda e, s=src, nb=nb, h=half: e.dma_start(
                    out=sb3(EH[:, h * 4096:h * 4096 + 4096], nb),
                    in_=s.rearrange("(i p) c -> p i c", p=128)), deps=dep)
                col = 18 if p_ == 0 else 19
                rtok = V(lambda e, c=col, h=half: e.tensor_reduce(
                    SM[:, c:c + 1], EH[:, h * 4096:h * 4096 + 4096], AX.X,
                    ALU.max, apply_absolute_value=True), deps=[ltok, prev])
                if p_ > 0:
                    rtok = V(lambda e: e.tensor_tensor(
                        SM[:, 18:19], SM[:, 18:19], SM[:, 19:20], ALU.max),
                        deps=[rtok])
                prev_half[half] = rtok
                prev = rtok
            sl, pcols, btok = pslot()
            t = T(lambda e, p=pcols: e.transpose(
                p[0:1, 0:128], SM[:, 18:19], s_eyef[:, :]), deps=[prev, btok])
            stok = S(lambda e, p=pcols: e.copy(s_sb[0:1, 0:128], p[0:1, 0:128]),
                     deps=[t])
            slot_tok[sl] = stok
            amax = V(lambda e, c=acc_col: e.tensor_reduce(
                SM[0:1, c:c + 1], s_sb[0:1, 0:128], AX.X, ALU.max), deps=[stok])
            return amax

        def quant_scalars(amax_tok, acc_col, scl_base, extra=()):
            t1 = V(lambda e, c=acc_col: e.tensor_scalar(
                SM[0:1, 24:25], SM[0:1, c:c + 1], 1.0 / 127.0, None, ALU.mult),
                deps=[amax_tok] + list(extra))
            t2 = V(lambda e, c=acc_col: e.reciprocal(SM[0:1, 25:26], SM[0:1, c:c + 1]),
                   deps=[amax_tok])
            t3 = V(lambda e: e.tensor_scalar(SM[0:1, 26:27], SM[0:1, 25:26], 127.0,
                                             None, ALU.mult), deps=[t2])
            t4 = V(lambda e: e.reciprocal(SM[0:1, 27:28], SM[0:1, 24:25]), deps=[t1])
            t5 = V(lambda e: e.tensor_scalar(SM[0:1, 28:29], SM[0:1, 27:28], 2.0,
                                             None, ALU.mult), deps=[t4])
            toks = []
            for j, (src_c, dep) in enumerate(((24, t1), (26, t3), (28, t5))):
                sl, pcols, btok = pslot()
                t = T(lambda e, p=pcols, c=src_c: e.matmul(
                    p[:, 0:1], s_ones[:, :], SM[0:1, c:c + 1], start=True, stop=True),
                    deps=[dep, ld2, btok])
                stok = S(lambda e, p=pcols, j=j, sb_=scl_base: e.copy(
                    scl[:, sb_ + j:sb_ + j + 1], p[:, 0:1]), deps=[t])
                slot_tok[sl] = stok
                toks.append(stok)
            return toks

        amax_fc = absmax_of(w_wfc, 4, 1024, 18, first_deps=aw_prev["casts"])
        sc_fc_toks = quant_scalars(amax_fc, 18, 0)
        amax_pj = absmax_of(w_wpj, 4, 256, 18, first_deps=[amax_fc])
        sc_pj_toks = quant_scalars(amax_pj, 18, 3, extra=sc_fc_toks)

        ZA = dict(wy=s_sb[:, 0:2048], z=s_sb[:, 2048:4096],
                  sg=EH[:, 0:2048], am=EH[:, 2048:4096], moff=0)
        ZB = dict(wy=EH[:, 4096:6144], z=EH[:, 6144:8192],
                  sg=PQ[:, 0:2048], am=xld[:, 0:2048], moff=2048)
        KINDS = (ZA, ZB)
        bq = V(lambda e: e.memset(SM[:, 92:93], -63.5))
        bq2 = V(lambda e: e.memset(SM[:, 93:94], -64.0))

        def quant_pair(srcs, inv_col, load_deps, cast_deps):
            lt = [None, None]
            vt = [None, None]
            st_ = [None, None]
            ct = [None, None]
            for j, Z in enumerate(KINDS):
                lt[j] = DS(lambda e, s=srcs[j], Z=Z: e.dma_start(
                    out=sb3(Z["wy"], s.shape[1]), in_=s), deps=load_deps[j])
                st_[j] = S(lambda e, Z=Z, c=inv_col: e.activation(
                    out=Z["wy"], in_=Z["wy"], func=AF.Copy,
                    scale=scl[:, c:c + 1]), deps=[lt[j]])
            for j, Z in enumerate(KINDS):
                vt[j] = V(lambda e, Z=Z: e.scalar_tensor_tensor(
                    Z["am"], Z["wy"], -1.0, Z["wy"], ALU.mult, ALU.max),
                    deps=[st_[j]])
                vt[j] = V(lambda e, Z=Z: e.tensor_copy(Z["z"], Z["am"]),
                          deps=[vt[j]])
            for k in (6, 5, 4, 3, 2, 1, 0):
                for j, Z in enumerate(KINDS):
                    st_[j] = S(lambda e, Z=Z: e.activation(
                        out=Z["sg"], in_=Z["z"], func=AF.Sign,
                        bias=SM[:, 92:93]), deps=[vt[j], bq])
                for j, Z in enumerate(KINDS):
                    vt[j] = V(lambda e, Z=Z, k=k: e.scalar_tensor_tensor(
                        Z["z"], Z["sg"], -float(2 ** (k - 1)), Z["z"],
                        ALU.mult, ALU.add), deps=[st_[j]])
            for j, Z in enumerate(KINDS):
                st_[j] = S(lambda e, Z=Z: e.activation(
                    out=Z["sg"], in_=Z["z"], func=AF.Sign,
                    bias=SM[:, 93:94]), deps=[vt[j], bq2])
            for j, Z in enumerate(KINDS):
                vt[j] = V(lambda e, Z=Z: e.tensor_tensor(
                    Z["z"], Z["am"], Z["z"], ALU.subtract), deps=[st_[j]])
                vt[j] = V(lambda e, Z=Z: e.scalar_tensor_tensor(
                    Z["z"], Z["sg"], 0.5, Z["z"], ALU.mult, ALU.add),
                    deps=[vt[j]])
            for j, Z in enumerate(KINDS):
                st_[j] = S(lambda e, Z=Z: e.activation(
                    out=Z["sg"], in_=Z["wy"], func=AF.Sign), deps=[vt[j]])
            for j, Z in enumerate(KINDS):
                t = V(lambda e, Z=Z: e.scalar_tensor_tensor(
                    Z["am"], Z["z"], 64.0, Z["sg"], ALU.add, ALU.mult),
                    deps=[st_[j]])
                t = V(lambda e, Z=Z: e.tensor_scalar(
                    Z["am"], Z["am"], 127.0, -128.0, ALU.min, ALU.max),
                    deps=[t])
                ct[j] = V(lambda e, Z=Z: e.tensor_copy(
                    mixA[:, Z["moff"]:Z["moff"] + 2048], Z["am"]),
                    deps=[t] + list(cast_deps[j]))
            return lt, ct

        wfc_wr_tok = [None] * FB
        wpj_wr_tok = [[None, None, None, None] for _ in range(8)]

        prev_piece = [[d for d in sc_fc_toks] + [d for d in sc_pj_toks],
                      [d for d in sc_fc_toks] + [d for d in sc_pj_toks] +
                      [aw_prev["last_pe"]]]
        prev_tr = [[aw_prev["last_pe"]], [aw_prev["last_pe"]]]
        last_tr_global = [None]

        for pr in range(8):
            srcs = [w_wfc[(2 * pr + j) * 256:(2 * pr + j + 1) * 256,
                          :].rearrange("(i p) c -> p i c", p=128)
                    for j in (0, 1)]
            lt, ct = quant_pair(srcs, 1,
                                [prev_piece[0], prev_piece[1]],
                                [prev_tr[0], prev_tr[1]])
            prev_piece = [[], []]
            prev_tr = [[], []]
            for j in (0, 1):
                p_ = 2 * pr + j
                moff = KINDS[j]["moff"]
                ctok = ct[j]
                for jj in range(2):
                    fb = p_ * 2 + jj
                    last = None
                    for cb in range(CB):
                        last = T(lambda e, jj=jj, cb=cb, m=moff: e.transpose(
                            P_trb[:, cb * 128:(cb + 1) * 128],
                            mixA[:, m + jj * 1024 + cb * 128:
                                 m + jj * 1024 + cb * 128 + 128],
                            s_eyeb[:, :]),
                            deps=([ctok, trb_tok[0]] if cb == 0 else []),
                            inc=(cb == 7))
                    half = fb % 2
                    stok = S(lambda e, h=half: e.copy(
                        wfc_buf[:, h * 1024:(h + 1) * 1024], P_trb[:, :]),
                        deps=[last, wfc_wr_tok[fb - 2] if fb >= 2 else None])
                    trb_tok[0] = stok
                    wfc_wr_tok[fb] = DS(lambda e, fb=fb, h=half: e.dma_start(
                        out=wfc_s[fb, :, :],
                        in_=wfc_buf[:, h * 1024:(h + 1) * 1024]), deps=[stok])
                    prev_piece[j].append(stok)
                    prev_tr[j] = [last]
                    last_tr_global[0] = last
                prev_piece[j].append(ctok)

        prev_piece[0] = prev_piece[0] + [d for d in sc_pj_toks]
        prev_piece[1] = prev_piece[1] + [d for d in sc_pj_toks]
        for db in range(8):
            srcs = [w_wpj[db * 128:(db + 1) * 128,
                          hf * 2048:(hf + 1) * 2048].rearrange(
                "(i p) c -> p i c", p=128) for hf in (0, 1)]
            lt, ct = quant_pair(srcs, 4,
                                [prev_piece[0], prev_piece[1]],
                                [prev_tr[0], prev_tr[1]])
            prev_piece = [[], []]
            prev_tr = [[], []]
            for hf in range(2):
                moff = KINDS[hf]["moff"]
                ctok = ct[hf]
                for grp in range(2):
                    g_abs = hf * 2 + grp
                    last = None
                    for jj in range(8):
                        fb = grp * 8 + jj
                        last = T(lambda e, fb=fb, jj=jj, m=moff: e.transpose(
                            P_trb[:, jj * 128:(jj + 1) * 128],
                            mixA[:, m + fb * 128:m + (fb + 1) * 128],
                            s_eyeb[:, :]),
                            deps=([ctok, trb_tok[0]] if jj == 0 else []),
                            inc=(jj == 7))
                    qslot = g_abs % 2
                    prev_use = None
                    if db > 0 or g_abs > 0:
                        idx = db * 4 + g_abs - 2
                        if idx >= 0:
                            prev_use = wpj_wr_tok[idx // 4][idx % 4]
                    stok = S(lambda e, q=qslot: e.copy(
                        wpj_buf[:, q * 1024:(q + 1) * 1024], P_trb[:, :]),
                        deps=[last, prev_use])
                    trb_tok[0] = stok
                    wpj_wr_tok[db][g_abs] = DS(
                        lambda e, db=db, g=g_abs, q=qslot: e.dma_start(
                            out=wpj_s[db, :, g * 1024:(g + 1) * 1024],
                            in_=wpj_buf[:, q * 1024:(q + 1) * 1024]),
                        deps=[stok])
                    prev_piece[hf].append(stok)
                    prev_tr[hf] = [last]
                    last_tr_global[0] = last
                prev_piece[hf].append(ctok)

        eps_tok = V(lambda e: e.memset(SM[:, 80:81], 1e-5))
        pro_s = ("s", sch.cnt["s"])
        pro_v = ("v", sch.cnt["v"])
        prologue_pe = last_tr_global[0]
        quant_last = prev_piece[0] + prev_piece[1]

        st = [dict() for _ in range(NCH)]
        ev = EH[:, 4096:8192]
        e_ = EH[:, 0:4096]
        ev3 = ev.rearrange("p (cb t) -> p cb t", t=512)
        e3 = e_.rearrange("p (cb t) -> p cb t", t=512)
        s3 = s_sb[:, :].rearrange("p (cb t) -> p cb t", t=512)
        xn3 = xn_ct[:, :].rearrange("p (cb t) -> p cb t", t=512)
        aggr2 = SM[:, 48:56].rearrange("p (t two) -> p t two", two=2)
        qp = [PQ[:, 0:512], PQ[:, 512:1024]]
        pp = [PQ[:, 1024:1536], PQ[:, 1536:2048]]

        def emit_xld(i):
            c = st[i]
            ldt = [None] * 4
            for tb in range(2):
                src = xd[i * TC + tb * 128: i * TC + (tb + 1) * 128, :]
                dep = [st[i - 1]["xn"][tb + 2]] if i > 0 else quant_last
                ldt[tb] = DA(lambda e, s=src, sl=tb % 2: e.dma_start(
                    out=xld[:, sl * 1024:(sl + 1) * 1024], in_=s), deps=dep)
            c["ldt"] = ldt

        def emit_LN(i):
            c = st[i]
            ldt = c["ldt"]
            xnt = [None] * 4
            for tb in range(4):
                if tb >= 2:
                    src = xd[i * TC + tb * 128: i * TC + (tb + 1) * 128, :]
                    ldt[tb] = DA(lambda e, s=src, sl=tb % 2: e.dma_start(
                        out=xld[:, sl * 1024:(sl + 1) * 1024], in_=s),
                        deps=[xnt[tb - 2]])
                t = V(lambda e, tb=tb, sl=tb % 2: e.bn_stats(
                    SM[:, tb * 12:tb * 12 + 6],
                    xld[:, sl * 1024:sl * 1024 + 512]),
                    deps=[ldt[tb]] + ([pro_v] if i == 0 else []))
                t = V(lambda e, tb=tb, sl=tb % 2: e.bn_stats(
                    SM[:, tb * 12 + 6:tb * 12 + 12],
                    xld[:, sl * 1024 + 512:sl * 1024 + 1024]), deps=[t])
                t = V(lambda e, tb=tb: e.bn_aggr(
                    SM[:, 48 + 2 * tb:50 + 2 * tb], SM[:, tb * 12:tb * 12 + 12]),
                    deps=[t])
                ts_ = S(lambda e, tb=tb: e.activation(
                    out=aggr2[:, tb:tb + 1, 1:2], in_=aggr2[:, tb:tb + 1, 1:2],
                    func=AF.Sqrt, bias=SM[:, 80:81]), deps=[t, eps_tok])
                t = V(lambda e, tb=tb: e.reciprocal(
                    aggr2[:, tb:tb + 1, 1:2], aggr2[:, tb:tb + 1, 1:2]), deps=[ts_])
                xnt[tb] = V(lambda e, tb=tb, sl=tb % 2: e.tensor_scalar(
                    xnn[:, tb * 1024:(tb + 1) * 1024],
                    xld[:, sl * 1024:(sl + 1) * 1024],
                    SM[:, 48 + 2 * tb:49 + 2 * tb], SM[:, 49 + 2 * tb:50 + 2 * tb],
                    ALU.subtract, ALU.mult),
                    deps=[t] + ([st[i - 1]["T_pe"]] if (i > 0 and tb == 0) else []))
            c["xn"] = xnt
            c["xn_last"] = xnt[3]

        def emit_T(i):
            c = st[i]
            prevc_nxt = prevc1 if i % 2 == 0 else prevc0
            fcdep = st[i - 1]["fc_pe"] if i > 0 else prologue_pe
            xrdep = st[i - 1]["xr"] if i > 0 else None
            for g in range(4):
                last = None
                for sl_ in range(8):
                    cb = 2 * g + sl_ // 4
                    tb = sl_ % 4
                    last = T(lambda e, sl_=sl_, cb=cb, tb=tb: e.transpose(
                        P_trb[:, sl_ * 128:(sl_ + 1) * 128],
                        xnn[:, tb * 1024 + cb * 128: tb * 1024 + cb * 128 + 128],
                        s_eyeb[:, :]),
                        deps=([c["xn"][3], trb_tok[0], fcdep, xrdep]
                              if sl_ == 0 else []),
                        inc=(sl_ == 7))
                stok = S(lambda e, g=g: e.copy(
                    xn_ct[:, 2 * g * 512:(2 * g + 2) * 512], P_trb[:, :]),
                    deps=[last])
                trb_tok[0] = stok
            c["xnct_s"] = stok
            c["T_pe"] = last
            c["pcv"] = V(lambda e, p=prevc_nxt: e.tensor_copy(
                p[:, :], xn3[:, :, 511:512]), deps=[stok])

        def emit_mix_kv(i):
            c = st[i]
            prevc_cur = prevc0 if i % 2 == 0 else prevc1
            dep0 = [c["xnct_s"], st[i - 1]["pcv"] if i > 0 else None]
            if i % 8 == 0:
                P(lambda e, p=prevc_cur: e.memset(p[:, :], 0.0), deps=dep0)
            d0tok = P(lambda e, p=prevc_cur: e.tensor_tensor(
                SM[:, 64:72], xn3[:, :, 0:1], p[:, :], ALU.subtract), deps=dep0)

            def mix_v(dst, tmix, first_deps):
                dst3 = dst[:, :].rearrange("p (cb t) -> p cb t", t=512)
                for cb in range(CB):
                    o = cb * 512
                    V(lambda e, o=o: e.tensor_tensor(
                        qp[0][:, 0:511], xn_ct[:, o + 1:o + 512],
                        xn_ct[:, o:o + 511], ALU.subtract),
                        deps=(first_deps + dep0 if cb == 0 else []))
                    V(lambda e, o=o, d=dst, tm=tmix, cb=cb: e.scalar_tensor_tensor(
                        d[:, o + 1:o + 512], qp[0][:, 0:511], tm[:, cb:cb + 1],
                        xn_ct[:, o:o + 511], ALU.mult, ALU.add))
                V(lambda e, tm=tmix: e.tensor_tensor(
                    SM[:, 72:80], SM[:, 64:72], tm[:, :], ALU.mult),
                    deps=[d0tok])
                return V(lambda e, d3=dst3, p=prevc_cur: e.tensor_tensor(
                    d3[:, :, 0:1], SM[:, 72:80], p[:, :], ALU.add))

            def mix_p(dst, tmix, first_deps):
                dst3 = dst[:, :].rearrange("p (cb t) -> p cb t", t=512)
                for cb in range(CB):
                    o = cb * 512
                    P(lambda e, o=o: e.tensor_tensor(
                        psc[:, 0:511], xn_ct[:, o + 1:o + 512],
                        xn_ct[:, o:o + 511], ALU.subtract),
                        deps=(first_deps if cb == 0 else []))
                    P(lambda e, tm=tmix, cb=cb: e.tensor_tensor(
                        psc[:, 512:1023], psc[:, 0:511],
                        tm[:, cb:cb + 1].broadcast_to([128, 511]), ALU.mult))
                    P(lambda e, o=o, d=dst: e.tensor_tensor(
                        d[:, o + 1:o + 512], psc[:, 512:1023],
                        xn_ct[:, o:o + 511], ALU.add))
                P(lambda e, tm=tmix: e.tensor_tensor(
                    SM[:, 84:92], SM[:, 64:72], tm[:, :], ALU.mult))
                return P(lambda e, d3=dst3, p=prevc_cur: e.tensor_tensor(
                    d3[:, :, 0:1], SM[:, 84:92], p[:, :], ALU.add))

            c["mix_p"] = mix_p
            c["xk"] = mix_v(mixA, vmk,
                            [st[i - 1]["r_pe"] if i > 0 else prologue_pe])
            c["xv"] = mix_v(mixB, vmv,
                            [st[i - 1]["vA_pe"] if i > 0 else prologue_pe])

        PREV_SLOT = {0: 6, 1: 7, 2: 5}

        def slab_phase(i, c, key, scratch, buf, wr_toks, rhs_of, evac_of,
                       first_mm_deps, rhs_dep_of=None):
            toks = [None] * 8
            prev = st[i - 1].get(key) if i > 0 else None
            rd = [None] * 8

            def prefetch(db):
                slotdep = toks[db - 3] if db >= 3 else (
                    prev[PREV_SLOT[db]] if prev else None)
                wdep = wr_toks[db] if i == 0 else None
                rd[db] = DS(lambda e, db=db: e.dma_start(
                    out=buf[:, (db % 3) * 1024:(db % 3 + 1) * 1024],
                    in_=scratch[db, :, :]), deps=[slotdep, wdep])

            for db in range(3):
                prefetch(db)
            evs_ = {}
            for db in range(8):
                sl, pcols, btok = pslot()
                last = None
                for cb in range(CB):
                    last = T(lambda e, db=db, cb=cb, p=pcols: e.matmul(
                        p, buf[:, (db % 3) * 1024 + cb * 128:
                               (db % 3) * 1024 + cb * 128 + 128],
                        rhs_of(cb), start=(cb == 0), stop=(cb == 7)),
                        deps=((first_mm_deps + [rd[db], btok]
                               if cb == 0 else []) +
                              ([rhs_dep_of(cb)] if rhs_dep_of is not None
                               and db == 0 else [])),
                        inc=(cb == 7))
                toks[db] = last
                if db + 3 < 8:
                    prefetch(db + 3)
                stok = evac_of(db, pcols, last)
                slot_tok[sl] = stok
                evs_[db] = stok
            c[key] = toks
            return toks, evs_, last

        def emit_kv(i):
            c = st[i]
            ekv = {}
            _, ev_e, lastk = slab_phase(
                i, c, "k_toks", wk_s, wk_buf, wk_wr_tok,
                lambda cb: mixA[:, cb * 512:(cb + 1) * 512],
                lambda db, p, last: S(
                    lambda e, db=db, p=p: e.activation(
                        out=e_[:, db * 512:(db + 1) * 512], in_=p, func=AF.Exp),
                    deps=[last, st[i - 1]["F"][db] if i > 0 else pro_v]),
                [c["xk"]] + ([pro_s] if i == 0 else []))
            for db in range(8):
                ekv[("e", db)] = ev_e[db]
            c["k_pe"] = lastk
            c["xr"] = c["mix_p"](mixA, vmr, [lastk])
            _, ev_v, lastv = slab_phase(
                i, c, "v_toks", wv_s, wv_buf, wv_wr_tok,
                lambda cb: mixB[:, cb * 512:(cb + 1) * 512],
                lambda db, p, last: S(
                    lambda e, db=db, p=p: e.copy(
                        ev[:, db * 512:(db + 1) * 512], p),
                    deps=[last, st[i - 1]["yP"][db] if i > 0 else pro_v]),
                [c["xv"]])
            for db in range(8):
                ekv[("v", db)] = ev_v[db]
            c["vA_pe"] = lastv
            c["ekv"] = ekv

        def emit_r(i):
            c = st[i]
            ekv = c["ekv"]
            _, ev_r, lastr = slab_phase(
                i, c, "r_toks", wr_s, wr_buf, wr_wr_tok,
                lambda cb: mixA[:, cb * 512:(cb + 1) * 512],
                lambda db, p, last: S(
                    lambda e, db=db, p=p: e.activation(
                        out=XS[:, db * 512:(db + 1) * 512], in_=p,
                        func=AF.Sigmoid), deps=[last]),
                [c["xr"]])
            for db in range(8):
                ekv[("r", db)] = ev_r[db]
            c["r_pe"] = lastr

        def emit_wkv(i):
            c = st[i]
            if i % 8 == 0:
                V(lambda e: e.memset(pcar[:, :], 0.0))
                V(lambda e: e.memset(qcar[:, :], 0.0))
            ekv = c["ekv"]
            t = V(lambda e: e.tensor_tensor(ev, e_, ev, ALU.mult),
                  deps=[ekv[("v", 7)], ekv[("e", 7)],
                        st[i - 1]["sgT_pe"] if i > 0 else None])
            t = V(lambda e: e.tensor_tensor(SM[:, 40:48], e3[:, :, 0:1], vU[:, :],
                                            ALU.mult), deps=[t])
            t = V(lambda e: e.tensor_tensor(SM[:, 40:48], SM[:, 40:48], qcar[:, :],
                                            ALU.add), deps=[t])
            t = V(lambda e: e.tensor_tensor(SM[:, 48:56], ev3[:, :, 0:1], vU[:, :],
                                            ALU.mult), deps=[t])
            den0_tok = t = V(lambda e: e.tensor_tensor(
                SM[:, 48:56], SM[:, 48:56], pcar[:, :], ALU.add), deps=[t])
            t = V(lambda e: e.tensor_tensor(SM[:, 56:64], qcar[:, :], vD[:, :],
                                            ALU.mult), deps=[t])
            t = V(lambda e: e.tensor_tensor(e3[:, :, 0:1], e3[:, :, 0:1],
                                            SM[:, 56:64], ALU.add), deps=[t])
            t = V(lambda e: e.tensor_tensor(SM[:, 56:64], pcar[:, :], vD[:, :],
                                            ALU.mult), deps=[t])
            t = V(lambda e: e.tensor_tensor(ev3[:, :, 0:1], ev3[:, :, 0:1],
                                            SM[:, 56:64], ALU.add), deps=[t])
            F_toks = [None] * CB
            G_toks = [None] * CB
            exp_toks = [None] * CB
            num_toks = [None] * CB
            yP = [None] * CB
            rw = [None] * CB
            sg_dep = st[i - 1]["sgT_pe"] if i > 0 else None

            def tail(cb):
                o = cb * 512
                yP[cb] = P(lambda e, o=o: e.tensor_tensor(
                    s_sb[:, o:o + 512], ev[:, o:o + 512], s_sb[:, o:o + 512],
                    ALU.mult), deps=[exp_toks[cb], num_toks[cb]])
                rw[cb] = P(lambda e, o=o: e.tensor_tensor(
                    XS[:, o:o + 512], s_sb[:, o:o + 512], XS[:, o:o + 512],
                    ALU.mult), deps=[yP[cb], ekv[("r", cb)]])

            for cb in range(CB):
                o = cb * 512
                es = e_[:, o:o + 512]; evs = ev[:, o:o + 512]
                par = cb % 2
                sq_ = V(lambda e, es=es, cb=cb, par=par: e.tensor_tensor_scan(
                    qp[par], vD[:, cb:cb + 1].broadcast_to([128, 512]), es, 0.0,
                    ALU.mult, ALU.add), deps=[t])
                sp_ = t = V(lambda e, evs=evs, cb=cb, par=par: e.tensor_tensor_scan(
                    pp[par], vD[:, cb:cb + 1].broadcast_to([128, 512]), evs, 0.0,
                    ALU.mult, ALU.add), deps=[sq_])
                F_toks[cb] = P(lambda e, es=es, cb=cb: e.tensor_tensor(
                    es, es, vU[:, cb:cb + 1].broadcast_to([128, 512]), ALU.mult),
                    deps=[sq_])
                G_toks[cb] = P(lambda e, evs=evs, cb=cb: e.tensor_tensor(
                    evs, evs, vU[:, cb:cb + 1].broadcast_to([128, 512]), ALU.mult),
                    deps=[sp_])
                t = V(lambda e, cb=cb, par=par: e.tensor_copy(
                    qcar[:, cb:cb + 1], qp[par][:, 511:512]), deps=[t])
                t = V(lambda e, cb=cb, par=par: e.tensor_copy(
                    pcar[:, cb:cb + 1], pp[par][:, 511:512]), deps=[t])
                t = V(lambda e, es=es, o=o, par=par: e.tensor_tensor(
                    s_sb[:, o + 1:o + 512], qp[par][:, 0:511], es[:, 1:512],
                    ALU.add), deps=[F_toks[cb], sg_dep] if cb == 0 else [F_toks[cb]])
                dcol = V(lambda e, o=o, cb=cb: e.tensor_copy(
                    s_sb[:, o:o + 1], SM[:, 40 + cb:41 + cb]), deps=[t])
                tl = S(lambda e, o=o: e.activation(
                    out=s_sb[:, o:o + 512], in_=s_sb[:, o:o + 512],
                    func=AF.Ln), deps=[dcol])
                exp_toks[cb] = S(lambda e, o=o: e.activation(
                    out=s_sb[:, o:o + 512], in_=s_sb[:, o:o + 512],
                    func=AF.Exp, scale=-1.0), deps=[tl])
                t = V(lambda e, evs=evs, par=par: e.tensor_tensor(
                    evs[:, 1:512], pp[par][:, 0:511], evs[:, 1:512], ALU.add),
                    deps=[G_toks[cb]])
                num_toks[cb] = t = V(lambda e, o=o, cb=cb: e.tensor_copy(
                    ev[:, o:o + 1], SM[:, 48 + cb:49 + cb]), deps=[t])
                if cb >= 1:
                    tail(cb - 1)
            tail(CB - 1)
            c["rwkv_cb"] = rw
            c["yP"] = yP
            c["F"] = F_toks

        def emit_Wo(i):
            c = st[i]
            _, ev_o, lasto = slab_phase(
                i, c, "wo_toks", wo_s, wo_buf, wo_wr_tok,
                lambda cb: XS[:, cb * 512:(cb + 1) * 512],
                lambda db, p, last: S(
                    lambda e, db=db, p=p: e.copy(
                        s_sb[:, db * 512:(db + 1) * 512], p),
                    deps=[last, st[i - 1]["sgT_pe"] if i > 0 else None]),
                [], rhs_dep_of=lambda cb: c["rwkv_cb"][cb])
            c["wo_pe"] = lasto
            c["s_ev"] = ev_o[7]

        def emit_mid_fc(i):
            c = st[i]
            sl, pcols, btok = pslot()
            lastm = None
            for cb in range(CB):
                lastm = T(lambda e, cb=cb, p=pcols: e.matmul(
                    p[0:16, :], a_fc_t[:, cb * 16:(cb + 1) * 16],
                    xn_ct[:, cb * 512:(cb + 1) * 512],
                    start=(cb == 0), stop=(cb == 7)),
                    deps=([c["xnct_s"], btok] if cb == 0 else []), inc=(cb == 7))
            mid_tok = S(lambda e, p=pcols: e.activation(
                out=mid_bf[:, :], in_=p[0:16, :], func=AF.Copy,
                scale=scl[0:16, 2:3]), deps=[lastm])
            slot_tok[sl] = mid_tok

            wfc_rd = [None] * FB
            fc_toks = [None] * FB

            def fc_prefetch(fb):
                slotdep = fc_toks[fb - 3] if fb >= 3 else (
                    st[i - 1]["fc_pe"] if i > 0 else None)
                wdep = wfc_wr_tok[fb] if i == 0 else None
                wfc_rd[fb] = DS(lambda e, fb=fb: e.dma_start(
                    out=wfc_buf[:, (fb % 3) * 1024:(fb % 3 + 1) * 1024],
                    in_=wfc_s[fb, :, :]), deps=[slotdep, wdep])

            for fb in range(3):
                fc_prefetch(fb)
            h_toks = [None] * FB
            for fb in range(FB):
                sl, pcols, btok = pslot()
                for cb in range(CB):
                    T(lambda e, fb=fb, cb=cb, p=pcols: e.matmul(
                        p, wfc_buf[:, (fb % 3) * 1024 + cb * 128:
                                   (fb % 3) * 1024 + cb * 128 + 128],
                        xn_ct[:, cb * 512:(cb + 1) * 512],
                        start=(cb == 0), stop=False),
                        deps=([wfc_rd[fb], btok] if cb == 0 else []), inc=False)
                lastf = T(lambda e, fb=fb, p=pcols: e.matmul(
                    p, b_fc_t[:, fb * 128:(fb + 1) * 128],
                    mid_bf[:, :], start=False, stop=True), deps=[mid_tok], inc=True)
                fc_toks[fb] = lastf
                if fb + 3 < FB:
                    fc_prefetch(fb + 3)
                htok = S(lambda e, fb=fb, p=pcols: e.activation(
                    out=h_bf[:, fb * 512:(fb + 1) * 512], in_=p, func=AF.Gelu,
                    scale=scl[:, 0:1], bias=bfc_c[:, fb:fb + 1]), deps=[lastf])
                slot_tok[sl] = htok
                h_toks[fb] = htok
            c["fc_pe"] = lastf
            c["h"] = h_toks

        def emit_mid2_proj(i):
            c = st[i]
            sl, pcols, btok = pslot()
            lastm2 = None
            for fb in range(FB):
                lastm2 = T(lambda e, fb=fb, p=pcols: e.matmul(
                    p[0:16, :], a_pj_t[:, fb * 16:(fb + 1) * 16],
                    h_bf[:, fb * 512:(fb + 1) * 512],
                    start=(fb == 0), stop=(fb == 31)),
                    deps=[c["h"][fb]] + ([btok] if fb == 0 else []), inc=(fb == 31))
            mid2_tok = S(lambda e, p=pcols: e.activation(
                out=mid2_bf[:, :], in_=p[0:16, :], func=AF.Copy,
                scale=scl[0:16, 5:6]), deps=[lastm2])
            slot_tok[sl] = mid2_tok

            sg_toks = [None] * 8
            wpj_rd = [None] * 16
            half_toks = [None] * 16

            def pj_prefetch(ha):
                db_, hh_ = ha // 2, ha % 2
                slotdep = half_toks[ha - 3] if ha >= 3 else (
                    st[i - 1]["proj_pe"] if i > 0 else None)
                wdeps = ([wpj_wr_tok[db_][2 * hh_], wpj_wr_tok[db_][2 * hh_ + 1]]
                         if i == 0 else [])
                wpj_rd[ha] = DS(lambda e, ha=ha, db_=db_, hh_=hh_: e.dma_start(
                    out=wpj_buf[:, (ha % 3) * 2048:(ha % 3 + 1) * 2048],
                    in_=wpj_s[db_, :, hh_ * 2048:(hh_ + 1) * 2048]),
                    deps=[slotdep] + wdeps)

            for ha in range(3):
                pj_prefetch(ha)
            for db in range(8):
                sl, pcols, btok = pslot()
                for fb in range(FB):
                    hh = fb // 16
                    ha = db * 2 + hh
                    fq = fb % 16
                    mm = T(lambda e, fb=fb, ha=ha, fq=fq, p=pcols: e.matmul(
                        p, wpj_buf[:, (ha % 3) * 2048 + fq * 128:
                                   (ha % 3) * 2048 + fq * 128 + 128],
                        h_bf[:, fb * 512:(fb + 1) * 512],
                        start=(fb == 0), stop=False),
                        deps=([wpj_rd[ha], btok] if fq == 0 else []),
                        inc=(fb == 15))
                    if fb == 15:
                        half_toks[db * 2] = mm
                        if db * 2 + 3 < 16:
                            pj_prefetch(db * 2 + 3)
                lastp = T(lambda e, db=db, p=pcols: e.matmul(
                    p, b_pj_t[:, db * 128:(db + 1) * 128],
                    mid2_bf[:, :], start=False, stop=True),
                    deps=[mid2_tok], inc=True)
                half_toks[db * 2 + 1] = lastp
                if db * 2 + 4 < 16:
                    pj_prefetch(db * 2 + 4)
                gtok = S(lambda e, db=db, p=pcols: e.activation(
                    out=g_s[:, (db % 2) * 512:(db % 2) * 512 + 512], in_=p,
                    func=AF.Silu, scale=scl[:, 3:4], bias=bpj_c[:, db:db + 1]),
                    deps=[lastp, sg_toks[db - 2] if db >= 2 else None,
                          c["s_ev"] if db == 0 else None])
                sgt = V(lambda e, db=db: e.tensor_tensor(
                    s_sb[:, db * 512:(db + 1) * 512],
                    s_sb[:, db * 512:(db + 1) * 512],
                    g_s[:, (db % 2) * 512:(db % 2) * 512 + 512], ALU.mult),
                    deps=[gtok])
                slot_tok[sl] = sgt
                sg_toks[db] = sgt
            c["proj_pe"] = lastp
            c["sg"] = sg_toks

        def emit_G(i):
            c = st[i]
            lastt = None
            stores = [None] * 4
            ldr = [None] * 4
            prev_stores = st[i - 1]["stores"] if i > 0 else None
            for tb in range(4):
                src = xd[i * TC + tb * 128: i * TC + (tb + 1) * 128, :]
                ldr[tb] = DA(lambda e, s=src, sl=tb % 2: e.dma_start(
                    out=xres[:, sl * 1024:(sl + 1) * 1024], in_=s),
                    deps=[stores[tb - 2]] if tb >= 2 else
                    ([prev_stores[tb + 2]] if prev_stores else []))
                add_toks = []
                for half in range(2):
                    sl, pcols, btok = pslot()
                    for k4 in range(4):
                        db = half * 4 + k4
                        lastt = T(lambda e, db=db, tb=tb, k4=k4, p=pcols:
                                  e.transpose(
                                      p[:, k4 * 128:(k4 + 1) * 128],
                                      s_sb[:, db * 512 + tb * 128:
                                           db * 512 + tb * 128 + 128],
                                      s_eyef[:, :]),
                                  deps=[c["sg"][db]] + ([btok] if k4 == 0 else []),
                                  inc=(k4 == 3))
                    atok = V(lambda e, tb=tb, half=half, p=pcols, sl2=tb % 2:
                             e.tensor_tensor(
                                 xres[:, sl2 * 1024 + half * 512:
                                      sl2 * 1024 + half * 512 + 512],
                                 p,
                                 xres[:, sl2 * 1024 + half * 512:
                                      sl2 * 1024 + half * 512 + 512],
                                 ALU.add), deps=[lastt, ldr[tb]])
                    slot_tok[sl] = atok
                    add_toks.append(atok)
                dsty = y[i * TC + tb * 128: i * TC + (tb + 1) * 128, :]
                stores[tb] = DA(lambda e, d=dsty, sl=tb % 2: e.dma_start(
                    out=d, in_=xres[:, sl * 1024:(sl + 1) * 1024]),
                    deps=[add_toks[-1]])
            c["sgT_pe"] = lastt
            c["stores"] = stores
            c["store"] = stores[3]

        emit_xld(0)
        emit_LN(0)
        emit_T(0)
        emit_mix_kv(0)
        for i in range(NCH):
            if i + 1 < NCH:
                emit_xld(i + 1)
            if i > 0:
                emit_wkv(i - 1)
            emit_kv(i)
            if i > 0:
                emit_Wo(i - 1)
            emit_r(i)
            if i > 0:
                emit_mid2_proj(i - 1)
            emit_mid_fc(i)
            if i + 1 < NCH:
                emit_LN(i + 1)
                emit_T(i + 1)
                emit_mix_kv(i + 1)
            if i > 0:
                emit_G(i - 1)
        emit_wkv(NCH - 1)
        emit_Wo(NCH - 1)
        emit_mid2_proj(NCH - 1)
        emit_G(NCH - 1)

        final_store = st[NCH - 1]["store"]


        with nc.Block() as block:
            def replay(handle, name):
                for waits, fn, sem, step in sch.ops[name]:
                    for k, v in waits:
                        handle.wait_ge(semmap[k], v)
                    inst = fn(handle)
                    if sem is not None:
                        inst.then_inc(semmap[sem], step)

            @block.tensor
            def _(e):
                replay(e, "tensor")

            @block.vector
            def _(e):
                replay(e, "vector")

            @block.gpsimd
            def _(e):
                replay(e, "gpsimd")

            @block.scalar
            def _(e):
                replay(e, "scalar")
                for i in range(NQA):
                    if sch.cnt[f"qa{i}"]:
                        e.wait_ge(semmap[f"qa{i}"], sch.cnt[f"qa{i}"])

            @block.sync
            def _(e):
                replay(e, "sync")
                for i in range(NQS):
                    if sch.cnt[f"qs{i}"]:
                        e.wait_ge(semmap[f"qs{i}"], sch.cnt[f"qs{i}"])
                for i in range(NQA):
                    if sch.cnt[f"qa{i}"]:
                        e.wait_ge(semmap[f"qa{i}"], sch.cnt[f"qa{i}"])

    return nc


_NC_CACHE = [None]


def kernel(L0_attn_x, L0_attn_time_decay, L0_attn_time_first, L0_attn_time_mix_k,
           L0_attn_time_mix_v, L0_attn_time_mix_r, L0_attn_Wk, L0_attn_Wv,
           L0_attn_Wr, L0_attn_Wo, L0_ffn_W_fc, L0_ffn_b_fc, L0_ffn_A_fc,
           L0_ffn_B_fc, L0_ffn_W_proj, L0_ffn_b_proj, L0_ffn_A_proj,
           L0_ffn_B_proj):
    x = np.ascontiguousarray(np.asarray(L0_attn_x, dtype=np.float32))
    f32 = lambda a, shape=None: np.ascontiguousarray(
        np.asarray(a, dtype=np.float32).reshape(shape)
        if shape else np.asarray(a, dtype=np.float32))
    common = {
        "w_td": f32(L0_attn_time_decay, (8, 128)),
        "w_tf": f32(L0_attn_time_first, (8, 128)),
        "w_tmk": f32(L0_attn_time_mix_k, (8, 128)),
        "w_tmv": f32(L0_attn_time_mix_v, (8, 128)),
        "w_tmr": f32(L0_attn_time_mix_r, (8, 128)),
        "w_wk": f32(L0_attn_Wk), "w_wv": f32(L0_attn_Wv),
        "w_wr": f32(L0_attn_Wr), "w_wo": f32(L0_attn_Wo),
        "w_wfc": f32(L0_ffn_W_fc), "w_bfc": f32(L0_ffn_b_fc, (32, 128)),
        "w_afc": f32(L0_ffn_A_fc), "w_bfcB": f32(L0_ffn_B_fc),
        "w_wpj": f32(L0_ffn_W_proj), "w_bpj": f32(L0_ffn_b_proj, (8, 128)),
        "w_apj": f32(L0_ffn_A_proj), "w_bpjB": f32(L0_ffn_B_proj),
        "eye_f": np.eye(128, dtype=np.float32),
        "eye_b": np.eye(128).astype(BF),
        "ones1": np.ones((1, 128), dtype=np.float32),
    }
    in_maps = []
    for c in range(NCORE):
        m = dict(common)
        m["xd"] = np.ascontiguousarray(x[2 * c:2 * c + 2].reshape(ROWS, C))
        in_maps.append(m)
    if _NC_CACHE[0] is None:
        _NC_CACHE[0] = build_nc()
    nc = _NC_CACHE[0]
    res = run_bass_kernel_spmd(nc, in_maps, core_ids=list(range(NCORE)))
    out = np.empty((16, 4096, 1024), dtype=np.float32)
    for c in range(NCORE):
        out[2 * c:2 * c + 2] = res.results[c]["y"].reshape(2, 4096, 1024)
    return out


if __name__ == "__main__":
    np.random.seed(0)
    build_nc()
    print("build ok")


# version 30 (final)
# speedup vs baseline: 1.0190x; 1.0190x over previous
import numpy as np
import ml_dtypes
import concourse.bass as bass
import concourse.mybir as mybir
from concourse.bass_utils import run_bass_kernel_spmd
from contextlib import ExitStack

F32 = mybir.dt.float32
BF16 = mybir.dt.bfloat16
AF = mybir.ActivationFunctionType
ALU = mybir.AluOpType
AX = mybir.AxisListType
BF = ml_dtypes.bfloat16

C = 1024
CB = 8
FF = 4096
FB = 32
R = 16
TC = 512
NB = 2
NCH = 16
NCORE = 8
ROWS = NB * 4096

NQS = 12
NQA = 6


class Sch:

    ENGS = ("tensor", "vector", "scalar", "sync", "gpsimd")

    def __init__(self):
        self.ops = {e: [] for e in self.ENGS}
        self.cnt = {"pe": 0, "v": 0, "s": 0, "g": 0}
        for i in range(NQS):
            self.cnt[f"qs{i}"] = 0
        for i in range(NQA):
            self.cnt[f"qa{i}"] = 0
        self.rr = {"qs": 0, "qa": 0}
        self.guar = {e: {} for e in self.ENGS}
        self.last = {"vector": None, "scalar": None, "gpsimd": None}

    def emit(self, eng, fn, deps=(), sem=None, step=1):
        deps = [d for d in deps if d is not None]
        if eng in self.last and self.last[eng] is not None:
            deps.append(self.last[eng])
        waits = []
        g = self.guar[eng]
        for k, v in deps:
            if v > g.get(k, 0):
                waits.append((k, v))
                g[k] = v
        tok = None
        if sem is not None:
            self.cnt[sem] += step
            tok = (sem, self.cnt[sem])
        self.ops[eng].append((waits, fn, sem, step))
        if eng in self.last and sem in ("v", "s", "g"):
            self.last[eng] = tok
        return tok

    def emit_dma(self, eng, fn, deps, pool):
        i = self.rr[pool] % (NQS if pool == "qs" else NQA)
        self.rr[pool] += 1
        sem = f"{pool}{i}"
        deps = list(deps) + [(sem, self.cnt[sem])] if self.cnt[sem] else list(deps)
        return self.emit(eng, fn, deps, sem, 16)


def build_nc():
    nc = bass.Bass("TRN2", target_bir_lowering=False, debug=False,
                   dynamic_dma_scratch_size=2048)

    def din(name, shape, dt=F32):
        return nc.dram_tensor(name, shape, dt, kind="ExternalInput").ap()

    xd = din("xd", [ROWS, C])
    y = nc.dram_tensor("y", [ROWS, C], F32, kind="ExternalOutput").ap()
    w_td = din("w_td", [8, 128]); w_tf = din("w_tf", [8, 128])
    w_tmk = din("w_tmk", [8, 128]); w_tmv = din("w_tmv", [8, 128])
    w_tmr = din("w_tmr", [8, 128])
    w_wk = din("w_wk", [C, C]); w_wv = din("w_wv", [C, C])
    w_wr = din("w_wr", [C, C]); w_wo = din("w_wo", [C, C])
    w_wfc = din("w_wfc", [FF, C]); w_bfc = din("w_bfc", [32, 128])
    w_afc = din("w_afc", [R, C]); w_bfcB = din("w_bfcB", [FF, R])
    w_wpj = din("w_wpj", [C, FF]); w_bpj = din("w_bpj", [8, 128])
    w_apj = din("w_apj", [R, FF]); w_bpjB = din("w_bpjB", [C, R])
    eye_f = din("eye_f", [128, 128]); eye_b = din("eye_b", [128, 128], BF16)
    ones1 = din("ones1", [1, 128])

    wk_s = nc.dram_tensor("wk_s", [8, 128, 1024], BF16, kind="Internal").ap()
    wv_s = nc.dram_tensor("wv_s", [8, 128, 1024], BF16, kind="Internal").ap()
    wo_s = nc.dram_tensor("wo_s", [8, 128, 1024], BF16, kind="Internal").ap()
    wr_s = nc.dram_tensor("wr_s", [8, 128, 1024], BF16, kind="Internal").ap()
    wfc_s = nc.dram_tensor("wfc_s", [FB, 128, 1024], BF16, kind="Internal").ap()
    wpj_s = nc.dram_tensor("wpj_s", [8, 128, FB * 128], BF16, kind="Internal").ap()

    ctx = ExitStack()
    with ctx:
        def sb(name, shape, dt=F32):
            return ctx.enter_context(nc.sbuf_tensor(name, shape, dt))

        a_fc_t = sb("a_fc_t", [128, 128], BF16)
        b_fc_t = sb("b_fc_t", [16, 4096], BF16)
        a_pj_t = sb("a_pj_t", [128, 512], BF16)
        b_pj_t = sb("b_pj_t", [16, 1024], BF16)
        vD = sb("vD", [128, 8]); vU = sb("vU", [128, 8])
        vmk = sb("vmk", [128, 8]); vmv = sb("vmv", [128, 8]); vmr = sb("vmr", [128, 8])
        bfc_c = sb("bfc_c", [128, 32]); bpj_c = sb("bpj_c", [128, 8])
        scl = sb("scl", [128, 8])
        s_eyef = sb("s_eyef", [128, 128]); s_eyeb = sb("s_eyeb", [128, 128], BF16)
        s_ones = sb("s_ones", [1, 128])
        wk_buf = sb("wk_buf", [128, 3072], BF16)
        wv_buf = sb("wv_buf", [128, 3072], BF16)
        wo_buf = sb("wo_buf", [128, 3072], BF16)
        wr_buf = sb("wr_buf", [128, 3072], BF16)
        wfc_buf = sb("wfc_buf", [128, 3072], BF16)
        wpj_buf = sb("wpj_buf", [128, 6144], BF16)
        xld = sb("xld", [128, 2048])
        xres = sb("xres", [128, 2048])
        xnn = sb("xnn", [128, 4096], BF16)
        XS = sb("XS", [128, 4096], BF16)
        xn_ct = sb("xn_ct", [128, 4096], BF16)
        mixA = sb("mixA", [128, 4096], BF16)
        mixB = sb("mixB", [128, 4096], BF16)
        EH = sb("EH", [128, 8192])
        s_sb = sb("s_sb", [128, 4096])
        PQ = sb("PQ", [128, 2048])
        SM = sb("SM", [128, 96])
        psc = sb("psc", [128, 1024])
        g_s = sb("g_s", [128, 1024])
        mid_bf = sb("mid_bf", [16, 512], BF16)
        mid2_bf = sb("mid2_bf", [16, 512], BF16)
        h_sb = sb("h_sb", [128, 16384], BF16)
        pcar = sb("pcar", [128, 8]); qcar = sb("qcar", [128, 8])
        prevc0 = sb("prevc0", [128, 8], BF16); prevc1 = sb("prevc1", [128, 8], BF16)

        P_big = ctx.enter_context(nc.psum_tensor("P_big", [128, 3584], F32))
        P_trb = ctx.enter_context(nc.psum_tensor("P_trb", [128, 1024], BF16))

        semmap = {}
        for nm in ["pe", "v", "s", "g"] + [f"qs{i}" for i in range(NQS)] + \
                  [f"qa{i}" for i in range(NQA)]:
            semmap[nm] = ctx.enter_context(nc.semaphore(f"sem_{nm}"))

        sch = Sch()

        def V(fn, deps=()):
            return sch.emit("vector", fn, deps, "v")

        def S(fn, deps=()):
            return sch.emit("scalar", fn, deps, "s")

        def T(fn, deps=(), inc=True):
            return sch.emit("tensor", fn, deps, "pe" if inc else None)

        def P(fn, deps=()):
            return sch.emit("gpsimd", fn, deps, "g")

        def DS(fn, deps=()):
            return sch.emit_dma("sync", fn, deps, "qs")

        def DA(fn, deps=()):
            return sch.emit_dma("scalar", fn, deps, "qa")

        h_bf = h_sb[:, :]

        def sb3(sbuf_ap, nblk):
            return sbuf_ap.rearrange("p (i c) -> p i c", i=nblk)

        slot_tok = [None] * 7
        slot_i = [0]

        def pslot():
            s = slot_i[0] % 7
            slot_i[0] += 1
            return s, P_big[:, s * 512:(s + 1) * 512], slot_tok[s]

        trb_tok = [None]

        ld0 = DS(lambda e: e.dma_start(out=s_eyef[:], in_=eye_f[:, :]))
        ld1 = DS(lambda e: e.dma_start(out=s_eyeb[:], in_=eye_b[:, :]))
        ld2 = DS(lambda e: e.dma_start(out=s_ones[:], in_=ones1[:, :]))

        nat = {}
        nat_specs = [
            ("td", w_td, PQ[0:8, 128:256]),
            ("tf", w_tf, PQ[0:8, 256:384]),
            ("tmk", w_tmk, PQ[0:8, 384:512]),
            ("tmv", w_tmv, PQ[0:8, 512:640]),
            ("tmr", w_tmr, PQ[0:8, 640:768]),
            ("bfc", w_bfc, PQ[0:32, 768:896]),
            ("bpj", w_bpj, PQ[0:8, 896:1024]),
            ("afc", w_afc, s_sb[0:16, 512:1536]),
            ("apj", w_apj, EH[0:16, 0:4096]),
        ]
        for name, src, dst in nat_specs:
            tok = DS(lambda e, s=src, d=dst: e.dma_start(out=d, in_=s[:, :]))
            nat[name] = (dst, tok)
        for name, src, dst in (
            ("bpjB", w_bpjB.rearrange("(db p) r -> p db r", p=128), xnn[:, 0:256]),
            ("bfcB", w_bfcB.rearrange("(fb p) r -> p fb r", p=128), s_sb[:, 0:512]),
        ):
            pass
        d3 = s_sb[:, 1536:1664].rearrange("p (a r) -> p a r", r=16)
        nat["bpjB"] = (s_sb[:, 1536:1664], DS(
            lambda e, s=w_bpjB.rearrange("(db p) r -> p db r", p=128), d=d3:
            e.dma_start(out=d, in_=s)))
        d3 = s_sb[:, 0:512].rearrange("p (a r) -> p a r", r=16)
        nat["bfcB"] = (s_sb[:, 0:512], DS(
            lambda e, s=w_bfcB.rearrange("(fb p) r -> p fb r", p=128), d=d3:
            e.dma_start(out=d, in_=s)))

        def small_tr(src_ap, n_in, dep, out_fn):
            sl, pcols, btok = pslot()
            t = T(lambda e, s=src_ap, p=pcols, n=n_in: e.transpose(
                p[:, 0:n], s, s_eyef[0:n, 0:n]), deps=[dep, ld0, btok])
            stok = out_fn(pcols, t)
            slot_tok[sl] = stok
            return stok

        dst, tok = nat["td"]
        e1 = small_tr(dst, 8, tok, lambda p, t: S(
            lambda e, p=p: e.activation(out=SM[:, 32:40], in_=p[:, 0:8], func=AF.Exp),
            deps=[t]))
        S(lambda e: e.activation(out=vD[:], in_=SM[:, 32:40], func=AF.Exp,
                                 scale=-1.0), deps=[e1])
        dst, tok = nat["tf"]
        small_tr(dst, 8, tok, lambda p, t: S(
            lambda e, p=p: e.activation(out=vU[:], in_=p[:, 0:8], func=AF.Exp),
            deps=[t]))
        for nm, dstv in (("tmk", vmk), ("tmv", vmv), ("tmr", vmr)):
            d_, tok = nat[nm]
            small_tr(d_, 8, tok, lambda p, t, dv=dstv: S(
                lambda e, p=p, dv=dv: e.copy(dv[:], p[:, 0:8]), deps=[t]))
        d_, tok = nat["bfc"]
        small_tr(d_, 32, tok, lambda p, t: S(
            lambda e, p=p: e.copy(bfc_c[:], p[:, 0:32]), deps=[t]))
        d_, tok = nat["bpj"]
        small_tr(d_, 8, tok, lambda p, t: S(
            lambda e, p=p: e.copy(bpj_c[:], p[:, 0:8]), deps=[t]))
        d_, tok = nat["afc"]
        for cb in range(CB):
            sl, pcols, btok = pslot()
            t = T(lambda e, p=pcols, cb=cb, d=d_: e.transpose(
                p[:, 0:16], d[0:16, cb * 128:(cb + 1) * 128], s_eyef[0:16, 0:16]),
                deps=[tok, ld0, btok])
            stok = S(lambda e, p=pcols, cb=cb: e.copy(
                a_fc_t[:, cb * 16:(cb + 1) * 16], p[:, 0:16]), deps=[t])
            slot_tok[sl] = stok
        d_, tok = nat["apj"]
        for fb in range(FB):
            sl, pcols, btok = pslot()
            t = T(lambda e, p=pcols, fb=fb, d=d_: e.transpose(
                p[:, 0:16], d[0:16, fb * 128:(fb + 1) * 128], s_eyef[0:16, 0:16]),
                deps=[tok, ld0, btok])
            stok = S(lambda e, p=pcols, fb=fb: e.copy(
                a_pj_t[:, fb * 16:(fb + 1) * 16], p[:, 0:16]), deps=[t])
            slot_tok[sl] = stok
        d_, tok = nat["bfcB"]
        for fb in range(FB):
            sl, pcols, btok = pslot()
            t = T(lambda e, p=pcols, fb=fb, d=d_: e.transpose(
                p[0:16, 0:128], d[:, fb * 16:(fb + 1) * 16], s_eyef[:, :]),
                deps=[tok, ld0, btok])
            stok = S(lambda e, p=pcols, fb=fb: e.copy(
                b_fc_t[:, fb * 128:(fb + 1) * 128], p[0:16, 0:128]), deps=[t])
            slot_tok[sl] = stok
        d_, tok = nat["bpjB"]
        for db in range(8):
            sl, pcols, btok = pslot()
            t = T(lambda e, p=pcols, db=db, d=d_: e.transpose(
                p[0:16, 0:128], d[:, db * 16:(db + 1) * 16], s_eyef[:, :]),
                deps=[tok, ld0, btok])
            stok = S(lambda e, p=pcols, db=db: e.copy(
                b_pj_t[:, db * 128:(db + 1) * 128], p[0:16, 0:128]), deps=[t])
            slot_tok[sl] = stok
        last_small_pe = t

        wk_wr_tok = [None] * 8
        wv_wr_tok = [None] * 8
        wo_wr_tok = [None] * 8
        wr_wr_tok = [None] * 8
        aw_prev = {"casts": [last_small_pe], "last_pe": None}

        def attn_weight(w_dram, resident, scratch=None, wr_toks=None,
                        prev_dma=None):
            ltok = DS(lambda e: e.dma_start(
                out=sb3(EH[:, 0:8192], 8),
                in_=w_dram.rearrange("(i p) c -> p i c", p=128)),
                deps=aw_prev["casts"])
            cA = V(lambda e: e.tensor_copy(mixA[:], EH[:, 0:4096]),
                   deps=[ltok, aw_prev["last_pe"]])
            cB = V(lambda e: e.tensor_copy(mixB[:], EH[:, 4096:8192]), deps=[ltok])

            def tile(i, cb):
                buf = mixA if i < 4 else mixB
                ii = i % 4
                return buf[:, ii * 1024 + cb * 128: ii * 1024 + cb * 128 + 128]

            last = None
            if True:
                buf = wo_buf
                for db in range(8):
                    for cb in range(CB):
                        last = T(lambda e, db=db, cb=cb: e.transpose(
                            P_trb[:, cb * 128:(cb + 1) * 128], tile(db, cb),
                            s_eyeb[:, :]),
                            deps=([cA, cB, ld1, trb_tok[0],
                                   wr_toks[db - 1] if db > 0 else None]
                                  if cb == 0 else []),
                            inc=(cb == 7))
                    stok = S(lambda e: e.copy(buf[:, 0:1024], P_trb[:, :]),
                             deps=[last, prev_dma if db == 0 else None])
                    trb_tok[0] = stok
                    wr_toks[db] = DS(lambda e, db=db, sc=scratch: e.dma_start(
                        out=sc[db, :, :], in_=buf[:, 0:1024]), deps=[stok])
            aw_prev["casts"] = [cA, cB]
            aw_prev["last_pe"] = last

        attn_weight(w_wk, None, wk_s, wk_wr_tok)
        attn_weight(w_wv, None, wv_s, wv_wr_tok, prev_dma=wk_wr_tok[7])
        attn_weight(w_wr, None, wr_s, wr_wr_tok, prev_dma=wv_wr_tok[7])
        attn_weight(w_wo, None, wo_s, wo_wr_tok, prev_dma=wr_wr_tok[7])

        def absmax_of(w_dram, npieces, piece_rows, acc_col, first_deps=()):
            prev_half = [None, None]
            prev = None
            pr2 = piece_rows // 2
            for p_ in range(npieces * 2):
                src = w_dram[p_ * pr2:(p_ + 1) * pr2, :]
                nb = pr2 // 128
                half = p_ % 2
                dep = ([prev_half[half]] if prev_half[half]
                       else list(first_deps))
                ltok = DS(lambda e, s=src, nb=nb, h=half: e.dma_start(
                    out=sb3(EH[:, h * 4096:h * 4096 + 4096], nb),
                    in_=s.rearrange("(i p) c -> p i c", p=128)), deps=dep)
                col = 18 if p_ == 0 else 19
                rtok = V(lambda e, c=col, h=half: e.tensor_reduce(
                    SM[:, c:c + 1], EH[:, h * 4096:h * 4096 + 4096], AX.X,
                    ALU.max, apply_absolute_value=True), deps=[ltok, prev])
                if p_ > 0:
                    rtok = V(lambda e: e.tensor_tensor(
                        SM[:, 18:19], SM[:, 18:19], SM[:, 19:20], ALU.max),
                        deps=[rtok])
                prev_half[half] = rtok
                prev = rtok
            sl, pcols, btok = pslot()
            t = T(lambda e, p=pcols: e.transpose(
                p[0:1, 0:128], SM[:, 18:19], s_eyef[:, :]), deps=[prev, btok])
            stok = S(lambda e, p=pcols: e.copy(s_sb[0:1, 0:128], p[0:1, 0:128]),
                     deps=[t])
            slot_tok[sl] = stok
            amax = V(lambda e, c=acc_col: e.tensor_reduce(
                SM[0:1, c:c + 1], s_sb[0:1, 0:128], AX.X, ALU.max), deps=[stok])
            return amax

        def quant_scalars(amax_tok, acc_col, scl_base, extra=()):
            t1 = V(lambda e, c=acc_col: e.tensor_scalar(
                SM[0:1, 24:25], SM[0:1, c:c + 1], 1.0 / 127.0, None, ALU.mult),
                deps=[amax_tok] + list(extra))
            t2 = V(lambda e, c=acc_col: e.reciprocal(SM[0:1, 25:26], SM[0:1, c:c + 1]),
                   deps=[amax_tok])
            t3 = V(lambda e: e.tensor_scalar(SM[0:1, 26:27], SM[0:1, 25:26], 127.0,
                                             None, ALU.mult), deps=[t2])
            t4 = V(lambda e: e.reciprocal(SM[0:1, 27:28], SM[0:1, 24:25]), deps=[t1])
            t5 = V(lambda e: e.tensor_scalar(SM[0:1, 28:29], SM[0:1, 27:28], 2.0,
                                             None, ALU.mult), deps=[t4])
            toks = []
            for j, (src_c, dep) in enumerate(((24, t1), (26, t3), (28, t5))):
                sl, pcols, btok = pslot()
                t = T(lambda e, p=pcols, c=src_c: e.matmul(
                    p[:, 0:1], s_ones[:, :], SM[0:1, c:c + 1], start=True, stop=True),
                    deps=[dep, ld2, btok])
                stok = S(lambda e, p=pcols, j=j, sb_=scl_base: e.copy(
                    scl[:, sb_ + j:sb_ + j + 1], p[:, 0:1]), deps=[t])
                slot_tok[sl] = stok
                toks.append(stok)
            return toks

        amax_fc = absmax_of(w_wfc, 4, 1024, 18, first_deps=aw_prev["casts"])
        sc_fc_toks = quant_scalars(amax_fc, 18, 0)
        amax_pj = absmax_of(w_wpj, 4, 256, 18, first_deps=[amax_fc])
        sc_pj_toks = quant_scalars(amax_pj, 18, 3, extra=sc_fc_toks)

        ZA = dict(wy=s_sb[:, 0:2048], z=s_sb[:, 2048:4096],
                  sg=EH[:, 0:2048], am=EH[:, 2048:4096], moff=0)
        ZB = dict(wy=EH[:, 4096:6144], z=EH[:, 6144:8192],
                  sg=PQ[:, 0:2048], am=xld[:, 0:2048], moff=2048)
        KINDS = (ZA, ZB)
        bq = V(lambda e: e.memset(SM[:, 92:93], -63.5))
        bq2 = V(lambda e: e.memset(SM[:, 93:94], -64.0))

        def quant_pair(srcs, inv_col, load_deps, cast_deps):
            lt = [None, None]
            vt = [None, None]
            st_ = [None, None]
            ct = [None, None]
            for j, Z in enumerate(KINDS):
                lt[j] = DS(lambda e, s=srcs[j], Z=Z: e.dma_start(
                    out=sb3(Z["wy"], s.shape[1]), in_=s), deps=load_deps[j])
                st_[j] = S(lambda e, Z=Z, c=inv_col: e.activation(
                    out=Z["wy"], in_=Z["wy"], func=AF.Copy,
                    scale=scl[:, c:c + 1]), deps=[lt[j]])
            for j, Z in enumerate(KINDS):
                vt[j] = V(lambda e, Z=Z: e.scalar_tensor_tensor(
                    Z["am"], Z["wy"], -1.0, Z["wy"], ALU.mult, ALU.max),
                    deps=[st_[j]])
                vt[j] = V(lambda e, Z=Z: e.tensor_copy(Z["z"], Z["am"]),
                          deps=[vt[j]])
            for k in (6, 5, 4, 3, 2, 1, 0):
                for j, Z in enumerate(KINDS):
                    st_[j] = S(lambda e, Z=Z: e.activation(
                        out=Z["sg"], in_=Z["z"], func=AF.Sign,
                        bias=SM[:, 92:93]), deps=[vt[j], bq])
                for j, Z in enumerate(KINDS):
                    vt[j] = V(lambda e, Z=Z, k=k: e.scalar_tensor_tensor(
                        Z["z"], Z["sg"], -float(2 ** (k - 1)), Z["z"],
                        ALU.mult, ALU.add), deps=[st_[j]])
            for j, Z in enumerate(KINDS):
                st_[j] = S(lambda e, Z=Z: e.activation(
                    out=Z["sg"], in_=Z["z"], func=AF.Sign,
                    bias=SM[:, 93:94]), deps=[vt[j], bq2])
            for j, Z in enumerate(KINDS):
                vt[j] = V(lambda e, Z=Z: e.tensor_tensor(
                    Z["z"], Z["am"], Z["z"], ALU.subtract), deps=[st_[j]])
                vt[j] = V(lambda e, Z=Z: e.scalar_tensor_tensor(
                    Z["z"], Z["sg"], 0.5, Z["z"], ALU.mult, ALU.add),
                    deps=[vt[j]])
            for j, Z in enumerate(KINDS):
                st_[j] = S(lambda e, Z=Z: e.activation(
                    out=Z["sg"], in_=Z["wy"], func=AF.Sign), deps=[vt[j]])
            for j, Z in enumerate(KINDS):
                t = V(lambda e, Z=Z: e.scalar_tensor_tensor(
                    Z["am"], Z["z"], 64.0, Z["sg"], ALU.add, ALU.mult),
                    deps=[st_[j]])
                t = V(lambda e, Z=Z: e.tensor_scalar(
                    Z["am"], Z["am"], 127.0, -128.0, ALU.min, ALU.max),
                    deps=[t])
                ct[j] = V(lambda e, Z=Z: e.tensor_copy(
                    mixA[:, Z["moff"]:Z["moff"] + 2048], Z["am"]),
                    deps=[t] + list(cast_deps[j]))
            return lt, ct

        wfc_wr_tok = [None] * FB
        wpj_wr_tok = [[None, None, None, None] for _ in range(8)]

        prev_piece = [[d for d in sc_fc_toks] + [d for d in sc_pj_toks],
                      [d for d in sc_fc_toks] + [d for d in sc_pj_toks] +
                      [aw_prev["last_pe"]]]
        prev_tr = [[aw_prev["last_pe"]], [aw_prev["last_pe"]]]
        last_tr_global = [None]

        for pr in range(8):
            srcs = [w_wfc[(2 * pr + j) * 256:(2 * pr + j + 1) * 256,
                          :].rearrange("(i p) c -> p i c", p=128)
                    for j in (0, 1)]
            lt, ct = quant_pair(srcs, 1,
                                [prev_piece[0], prev_piece[1]],
                                [prev_tr[0], prev_tr[1]])
            prev_piece = [[], []]
            prev_tr = [[], []]
            for j in (0, 1):
                p_ = 2 * pr + j
                moff = KINDS[j]["moff"]
                ctok = ct[j]
                for jj in range(2):
                    fb = p_ * 2 + jj
                    last = None
                    for cb in range(CB):
                        last = T(lambda e, jj=jj, cb=cb, m=moff: e.transpose(
                            P_trb[:, cb * 128:(cb + 1) * 128],
                            mixA[:, m + jj * 1024 + cb * 128:
                                 m + jj * 1024 + cb * 128 + 128],
                            s_eyeb[:, :]),
                            deps=([ctok, trb_tok[0]] if cb == 0 else []),
                            inc=(cb == 7))
                    half = fb % 2
                    stok = S(lambda e, h=half: e.copy(
                        wfc_buf[:, h * 1024:(h + 1) * 1024], P_trb[:, :]),
                        deps=[last, wfc_wr_tok[fb - 2] if fb >= 2 else None])
                    trb_tok[0] = stok
                    wfc_wr_tok[fb] = DS(lambda e, fb=fb, h=half: e.dma_start(
                        out=wfc_s[fb, :, :],
                        in_=wfc_buf[:, h * 1024:(h + 1) * 1024]), deps=[stok])
                    prev_piece[j].append(stok)
                    prev_tr[j] = [last]
                    last_tr_global[0] = last
                prev_piece[j].append(ctok)

        prev_piece[0] = prev_piece[0] + [d for d in sc_pj_toks]
        prev_piece[1] = prev_piece[1] + [d for d in sc_pj_toks]
        for db in range(8):
            srcs = [w_wpj[db * 128:(db + 1) * 128,
                          hf * 2048:(hf + 1) * 2048].rearrange(
                "(i p) c -> p i c", p=128) for hf in (0, 1)]
            lt, ct = quant_pair(srcs, 4,
                                [prev_piece[0], prev_piece[1]],
                                [prev_tr[0], prev_tr[1]])
            prev_piece = [[], []]
            prev_tr = [[], []]
            for hf in range(2):
                moff = KINDS[hf]["moff"]
                ctok = ct[hf]
                for grp in range(2):
                    g_abs = hf * 2 + grp
                    last = None
                    for jj in range(8):
                        fb = grp * 8 + jj
                        last = T(lambda e, fb=fb, jj=jj, m=moff: e.transpose(
                            P_trb[:, jj * 128:(jj + 1) * 128],
                            mixA[:, m + fb * 128:m + (fb + 1) * 128],
                            s_eyeb[:, :]),
                            deps=([ctok, trb_tok[0]] if jj == 0 else []),
                            inc=(jj == 7))
                    qslot = g_abs % 2
                    prev_use = None
                    if db > 0 or g_abs > 0:
                        idx = db * 4 + g_abs - 2
                        if idx >= 0:
                            prev_use = wpj_wr_tok[idx // 4][idx % 4]
                    stok = S(lambda e, q=qslot: e.copy(
                        wpj_buf[:, q * 1024:(q + 1) * 1024], P_trb[:, :]),
                        deps=[last, prev_use])
                    trb_tok[0] = stok
                    wpj_wr_tok[db][g_abs] = DS(
                        lambda e, db=db, g=g_abs, q=qslot: e.dma_start(
                            out=wpj_s[db, :, g * 1024:(g + 1) * 1024],
                            in_=wpj_buf[:, q * 1024:(q + 1) * 1024]),
                        deps=[stok])
                    prev_piece[hf].append(stok)
                    prev_tr[hf] = [last]
                    last_tr_global[0] = last
                prev_piece[hf].append(ctok)

        eps_tok = V(lambda e: e.memset(SM[:, 80:81], 1e-5))
        pro_s = ("s", sch.cnt["s"])
        pro_v = ("v", sch.cnt["v"])
        prologue_pe = last_tr_global[0]
        quant_last = prev_piece[0] + prev_piece[1]

        st = [dict() for _ in range(NCH)]
        ev = EH[:, 4096:8192]
        e_ = EH[:, 0:4096]
        ev3 = ev.rearrange("p (cb t) -> p cb t", t=512)
        e3 = e_.rearrange("p (cb t) -> p cb t", t=512)
        s3 = s_sb[:, :].rearrange("p (cb t) -> p cb t", t=512)
        xn3 = xn_ct[:, :].rearrange("p (cb t) -> p cb t", t=512)
        aggr2 = SM[:, 48:56].rearrange("p (t two) -> p t two", two=2)
        qp = [PQ[:, 0:512], PQ[:, 512:1024]]
        pp = [PQ[:, 1024:1536], PQ[:, 1536:2048]]

        def emit_xld(i):
            c = st[i]
            ldt = [None] * 4
            for tb in range(2):
                src = xd[i * TC + tb * 128: i * TC + (tb + 1) * 128, :]
                dep = [st[i - 1]["xn"][tb + 2]] if i > 0 else quant_last
                ldt[tb] = DA(lambda e, s=src, sl=tb % 2: e.dma_start(
                    out=xld[:, sl * 1024:(sl + 1) * 1024], in_=s), deps=dep)
            c["ldt"] = ldt

        def emit_LN(i):
            c = st[i]
            ldt = c["ldt"]
            xnt = [None] * 4
            for tb in range(4):
                if tb >= 2:
                    src = xd[i * TC + tb * 128: i * TC + (tb + 1) * 128, :]
                    ldt[tb] = DA(lambda e, s=src, sl=tb % 2: e.dma_start(
                        out=xld[:, sl * 1024:(sl + 1) * 1024], in_=s),
                        deps=[xnt[tb - 2]])
                t = V(lambda e, tb=tb, sl=tb % 2: e.bn_stats(
                    SM[:, tb * 12:tb * 12 + 6],
                    xld[:, sl * 1024:sl * 1024 + 512]),
                    deps=[ldt[tb]] + ([pro_v] if i == 0 else []))
                t = V(lambda e, tb=tb, sl=tb % 2: e.bn_stats(
                    SM[:, tb * 12 + 6:tb * 12 + 12],
                    xld[:, sl * 1024 + 512:sl * 1024 + 1024]), deps=[t])
                t = V(lambda e, tb=tb: e.bn_aggr(
                    SM[:, 48 + 2 * tb:50 + 2 * tb], SM[:, tb * 12:tb * 12 + 12]),
                    deps=[t])
                ts_ = S(lambda e, tb=tb: e.activation(
                    out=aggr2[:, tb:tb + 1, 1:2], in_=aggr2[:, tb:tb + 1, 1:2],
                    func=AF.Sqrt, bias=SM[:, 80:81]), deps=[t, eps_tok])
                t = V(lambda e, tb=tb: e.reciprocal(
                    aggr2[:, tb:tb + 1, 1:2], aggr2[:, tb:tb + 1, 1:2]), deps=[ts_])
                xnt[tb] = V(lambda e, tb=tb, sl=tb % 2: e.tensor_scalar(
                    xnn[:, tb * 1024:(tb + 1) * 1024],
                    xld[:, sl * 1024:(sl + 1) * 1024],
                    SM[:, 48 + 2 * tb:49 + 2 * tb], SM[:, 49 + 2 * tb:50 + 2 * tb],
                    ALU.subtract, ALU.mult),
                    deps=[t] + ([st[i - 1]["T_pe"]] if (i > 0 and tb == 0) else []))
            c["xn"] = xnt
            c["xn_last"] = xnt[3]

        def emit_T(i):
            c = st[i]
            prevc_nxt = prevc1 if i % 2 == 0 else prevc0
            fcdep = st[i - 1]["fc_pe"] if i > 0 else prologue_pe
            xrdep = st[i - 1]["xr"] if i > 0 else None
            for g in range(4):
                last = None
                for sl_ in range(8):
                    cb = 2 * g + sl_ // 4
                    tb = sl_ % 4
                    last = T(lambda e, sl_=sl_, cb=cb, tb=tb: e.transpose(
                        P_trb[:, sl_ * 128:(sl_ + 1) * 128],
                        xnn[:, tb * 1024 + cb * 128: tb * 1024 + cb * 128 + 128],
                        s_eyeb[:, :]),
                        deps=([c["xn"][3], trb_tok[0], fcdep, xrdep]
                              if sl_ == 0 else []),
                        inc=(sl_ == 7))
                stok = S(lambda e, g=g: e.copy(
                    xn_ct[:, 2 * g * 512:(2 * g + 2) * 512], P_trb[:, :]),
                    deps=[last])
                trb_tok[0] = stok
            c["xnct_s"] = stok
            c["T_pe"] = last
            c["pcv"] = V(lambda e, p=prevc_nxt: e.tensor_copy(
                p[:, :], xn3[:, :, 511:512]), deps=[stok])

        def emit_mix_kv(i):
            c = st[i]
            prevc_cur = prevc0 if i % 2 == 0 else prevc1
            dep0 = [c["xnct_s"], st[i - 1]["pcv"] if i > 0 else None]
            if i % 8 == 0:
                P(lambda e, p=prevc_cur: e.memset(p[:, :], 0.0), deps=dep0)
            d0tok = P(lambda e, p=prevc_cur: e.tensor_tensor(
                SM[:, 64:72], xn3[:, :, 0:1], p[:, :], ALU.subtract), deps=dep0)

            def mix_v(dst, tmix, first_deps):
                dst3 = dst[:, :].rearrange("p (cb t) -> p cb t", t=512)
                for cb in range(CB):
                    o = cb * 512
                    V(lambda e, o=o: e.tensor_tensor(
                        qp[0][:, 0:511], xn_ct[:, o + 1:o + 512],
                        xn_ct[:, o:o + 511], ALU.subtract),
                        deps=(first_deps + dep0 if cb == 0 else []))
                    V(lambda e, o=o, d=dst, tm=tmix, cb=cb: e.scalar_tensor_tensor(
                        d[:, o + 1:o + 512], qp[0][:, 0:511], tm[:, cb:cb + 1],
                        xn_ct[:, o:o + 511], ALU.mult, ALU.add))
                V(lambda e, tm=tmix: e.tensor_tensor(
                    SM[:, 72:80], SM[:, 64:72], tm[:, :], ALU.mult),
                    deps=[d0tok])
                return V(lambda e, d3=dst3, p=prevc_cur: e.tensor_tensor(
                    d3[:, :, 0:1], SM[:, 72:80], p[:, :], ALU.add))

            def mix_p(dst, tmix, first_deps):
                dst3 = dst[:, :].rearrange("p (cb t) -> p cb t", t=512)
                for cb in range(CB):
                    o = cb * 512
                    P(lambda e, o=o: e.tensor_tensor(
                        psc[:, 0:511], xn_ct[:, o + 1:o + 512],
                        xn_ct[:, o:o + 511], ALU.subtract),
                        deps=(first_deps if cb == 0 else []))
                    P(lambda e, tm=tmix, cb=cb: e.tensor_tensor(
                        psc[:, 512:1023], psc[:, 0:511],
                        tm[:, cb:cb + 1].broadcast_to([128, 511]), ALU.mult))
                    P(lambda e, o=o, d=dst: e.tensor_tensor(
                        d[:, o + 1:o + 512], psc[:, 512:1023],
                        xn_ct[:, o:o + 511], ALU.add))
                P(lambda e, tm=tmix: e.tensor_tensor(
                    SM[:, 84:92], SM[:, 64:72], tm[:, :], ALU.mult))
                return P(lambda e, d3=dst3, p=prevc_cur: e.tensor_tensor(
                    d3[:, :, 0:1], SM[:, 84:92], p[:, :], ALU.add))

            c["mix_p"] = mix_p
            c["xk"] = mix_v(mixA, vmk,
                            [st[i - 1]["r_pe"] if i > 0 else prologue_pe])
            c["xv"] = mix_v(mixB, vmv,
                            [st[i - 1]["vA_pe"] if i > 0 else prologue_pe])

        PREV_SLOT = {0: 6, 1: 7, 2: 5}

        def slab_phase(i, c, key, scratch, buf, wr_toks, rhs_of, evac_of,
                       first_mm_deps, rhs_dep_of=None, after_db=None):
            toks = [None] * 8
            prev = st[i - 1].get(key) if i > 0 else None
            rd = [None] * 8

            def prefetch(db):
                slotdep = toks[db - 3] if db >= 3 else (
                    prev[PREV_SLOT[db]] if prev else None)
                wdep = wr_toks[db] if i == 0 else None
                rd[db] = DS(lambda e, db=db: e.dma_start(
                    out=buf[:, (db % 3) * 1024:(db % 3 + 1) * 1024],
                    in_=scratch[db, :, :]), deps=[slotdep, wdep])

            for db in range(3):
                prefetch(db)
            evs_ = {}
            for db in range(8):
                sl, pcols, btok = pslot()
                last = None
                for cb in range(CB):
                    last = T(lambda e, db=db, cb=cb, p=pcols: e.matmul(
                        p, buf[:, (db % 3) * 1024 + cb * 128:
                               (db % 3) * 1024 + cb * 128 + 128],
                        rhs_of(cb), start=(cb == 0), stop=(cb == 7)),
                        deps=((first_mm_deps + [rd[db], btok]
                               if cb == 0 else []) +
                              ([rhs_dep_of(cb)] if rhs_dep_of is not None
                               and db == 0 else [])),
                        inc=(cb == 7))
                toks[db] = last
                if db + 3 < 8:
                    prefetch(db + 3)
                if after_db is not None:
                    after_db(db)
                stok = evac_of(db, pcols, last)
                slot_tok[sl] = stok
                evs_[db] = stok
            c[key] = toks
            return toks, evs_, last

        def emit_kv(i, wkv_chain=None, wkv_drain=None):
            c = st[i]
            ekv = {}
            _, ev_e, lastk = slab_phase(
                i, c, "k_toks", wk_s, wk_buf, wk_wr_tok,
                lambda cb: mixA[:, cb * 512:(cb + 1) * 512],
                lambda db, p, last: S(
                    lambda e, db=db, p=p: e.activation(
                        out=e_[:, db * 512:(db + 1) * 512], in_=p, func=AF.Exp),
                    deps=[last, st[i - 1]["F"][db] if i > 0 else pro_v]),
                [c["xk"]] + ([pro_s] if i == 0 else []),
                after_db=wkv_chain)
            if wkv_drain is not None:
                wkv_drain()
            for db in range(8):
                ekv[("e", db)] = ev_e[db]
            c["k_pe"] = lastk
            c["xr"] = c["mix_p"](mixA, vmr, [lastk])
            _, ev_v, lastv = slab_phase(
                i, c, "v_toks", wv_s, wv_buf, wv_wr_tok,
                lambda cb: mixB[:, cb * 512:(cb + 1) * 512],
                lambda db, p, last: S(
                    lambda e, db=db, p=p: e.copy(
                        ev[:, db * 512:(db + 1) * 512], p),
                    deps=[last, st[i - 1]["yP"][db] if i > 0 else pro_v]),
                [c["xv"]])
            for db in range(8):
                ekv[("v", db)] = ev_v[db]
            c["vA_pe"] = lastv
            c["ekv"] = ekv

        def emit_r(i):
            c = st[i]
            ekv = c["ekv"]
            _, ev_r, lastr = slab_phase(
                i, c, "r_toks", wr_s, wr_buf, wr_wr_tok,
                lambda cb: mixA[:, cb * 512:(cb + 1) * 512],
                lambda db, p, last: S(
                    lambda e, db=db, p=p: e.activation(
                        out=XS[:, db * 512:(db + 1) * 512], in_=p,
                        func=AF.Sigmoid), deps=[last]),
                [c["xr"]])
            for db in range(8):
                ekv[("r", db)] = ev_r[db]
            c["r_pe"] = lastr

        def emit_wkv(i, standalone=False):
            c = st[i]
            if i % 8 == 0:
                V(lambda e: e.memset(pcar[:, :], 0.0))
                V(lambda e: e.memset(qcar[:, :], 0.0))
            ekv = c["ekv"]
            t = V(lambda e: e.tensor_tensor(ev, e_, ev, ALU.mult),
                  deps=[ekv[("v", 7)], ekv[("e", 7)],
                        st[i - 1]["sgT_pe"] if i > 0 else None])
            t = V(lambda e: e.tensor_tensor(SM[:, 40:48], e3[:, :, 0:1], vU[:, :],
                                            ALU.mult), deps=[t])
            t = V(lambda e: e.tensor_tensor(SM[:, 40:48], SM[:, 40:48], qcar[:, :],
                                            ALU.add), deps=[t])
            t = V(lambda e: e.tensor_tensor(SM[:, 48:56], ev3[:, :, 0:1], vU[:, :],
                                            ALU.mult), deps=[t])
            den0_tok = t = V(lambda e: e.tensor_tensor(
                SM[:, 48:56], SM[:, 48:56], pcar[:, :], ALU.add), deps=[t])
            t = V(lambda e: e.tensor_tensor(SM[:, 56:64], qcar[:, :], vD[:, :],
                                            ALU.mult), deps=[t])
            t = V(lambda e: e.tensor_tensor(e3[:, :, 0:1], e3[:, :, 0:1],
                                            SM[:, 56:64], ALU.add), deps=[t])
            t = V(lambda e: e.tensor_tensor(SM[:, 56:64], pcar[:, :], vD[:, :],
                                            ALU.mult), deps=[t])
            t = V(lambda e: e.tensor_tensor(ev3[:, :, 0:1], ev3[:, :, 0:1],
                                            SM[:, 56:64], ALU.add), deps=[t])
            F_toks = [None] * CB
            G_toks = [None] * CB
            exp_toks = [None] * CB
            num_toks = [None] * CB
            yP = [None] * CB
            rw = [None] * CB
            eA = [None] * CB
            sg_dep = st[i - 1]["sgT_pe"] if i > 0 else None
            tc = [t]

            def tail(cb):
                o = cb * 512
                yP[cb] = P(lambda e, o=o: e.tensor_tensor(
                    s_sb[:, o:o + 512], ev[:, o:o + 512], s_sb[:, o:o + 512],
                    ALU.mult), deps=[exp_toks[cb], num_toks[cb]])
                rw[cb] = P(lambda e, o=o: e.tensor_tensor(
                    XS[:, o:o + 512], s_sb[:, o:o + 512], XS[:, o:o + 512],
                    ALU.mult), deps=[yP[cb], ekv[("r", cb)]])

            def chain(cb):
                t = tc[0]
                o = cb * 512
                es = e_[:, o:o + 512]; evs = ev[:, o:o + 512]
                par = cb % 2
                sq_ = V(lambda e, es=es, cb=cb, par=par: e.tensor_tensor_scan(
                    qp[par], vD[:, cb:cb + 1].broadcast_to([128, 512]), es, 0.0,
                    ALU.mult, ALU.add), deps=[t])
                sp_ = t = V(lambda e, evs=evs, cb=cb, par=par: e.tensor_tensor_scan(
                    pp[par], vD[:, cb:cb + 1].broadcast_to([128, 512]), evs, 0.0,
                    ALU.mult, ALU.add), deps=[sq_])
                F_toks[cb] = P(lambda e, es=es, cb=cb: e.tensor_tensor(
                    es, es, vU[:, cb:cb + 1].broadcast_to([128, 512]), ALU.mult),
                    deps=[sq_])
                G_toks[cb] = P(lambda e, evs=evs, cb=cb: e.tensor_tensor(
                    evs, evs, vU[:, cb:cb + 1].broadcast_to([128, 512]), ALU.mult),
                    deps=[sp_])
                t = V(lambda e, cb=cb, par=par: e.tensor_copy(
                    qcar[:, cb:cb + 1], qp[par][:, 511:512]), deps=[t])
                t = V(lambda e, cb=cb, par=par: e.tensor_copy(
                    pcar[:, cb:cb + 1], pp[par][:, 511:512]), deps=[t])
                t = V(lambda e, es=es, o=o, par=par: e.tensor_tensor(
                    s_sb[:, o + 1:o + 512], qp[par][:, 0:511], es[:, 1:512],
                    ALU.add), deps=[F_toks[cb], sg_dep] if cb == 0 else [F_toks[cb]])
                dcol = V(lambda e, o=o, cb=cb: e.tensor_copy(
                    s_sb[:, o:o + 1], SM[:, 40 + cb:41 + cb]), deps=[t])
                eA[cb] = dcol
                tl = S(lambda e, o=o: e.activation(
                    out=s_sb[:, o:o + 512], in_=s_sb[:, o:o + 512],
                    func=AF.Ln), deps=[dcol])
                exp_toks[cb] = S(lambda e, o=o: e.activation(
                    out=s_sb[:, o:o + 512], in_=s_sb[:, o:o + 512],
                    func=AF.Exp, scale=-1.0), deps=[tl])
                t = V(lambda e, evs=evs, par=par: e.tensor_tensor(
                    evs[:, 1:512], pp[par][:, 0:511], evs[:, 1:512], ALU.add),
                    deps=[G_toks[cb]])
                num_toks[cb] = t = V(lambda e, o=o, cb=cb: e.tensor_copy(
                    ev[:, o:o + 1], SM[:, 48 + cb:49 + cb]), deps=[t])
                if cb >= 1:
                    tail(cb - 1)
                tc[0] = t

            def drain():
                tail(CB - 1)

            c["rwkv_cb"] = rw
            c["yP"] = yP
            c["F"] = eA
            if standalone:
                for cb in range(CB):
                    chain(cb)
                drain()
                return None, None
            return chain, drain

        def emit_Wo(i):
            c = st[i]
            _, ev_o, lasto = slab_phase(
                i, c, "wo_toks", wo_s, wo_buf, wo_wr_tok,
                lambda cb: XS[:, cb * 512:(cb + 1) * 512],
                lambda db, p, last: S(
                    lambda e, db=db, p=p: e.copy(
                        s_sb[:, db * 512:(db + 1) * 512], p),
                    deps=[last, st[i - 1]["sgT_pe"] if i > 0 else None]),
                [], rhs_dep_of=lambda cb: c["rwkv_cb"][cb])
            c["wo_pe"] = lasto
            c["s_ev"] = ev_o[7]

        def emit_mid_fc(i):
            c = st[i]
            sl, pcols, btok = pslot()
            lastm = None
            for cb in range(CB):
                lastm = T(lambda e, cb=cb, p=pcols: e.matmul(
                    p[0:16, :], a_fc_t[:, cb * 16:(cb + 1) * 16],
                    xn_ct[:, cb * 512:(cb + 1) * 512],
                    start=(cb == 0), stop=(cb == 7)),
                    deps=([c["xnct_s"], btok] if cb == 0 else []), inc=(cb == 7))
            mid_tok = S(lambda e, p=pcols: e.activation(
                out=mid_bf[:, :], in_=p[0:16, :], func=AF.Copy,
                scale=scl[0:16, 2:3]), deps=[lastm])
            slot_tok[sl] = mid_tok

            wfc_rd = [None] * FB
            fc_toks = [None] * FB

            def fc_prefetch(fb):
                slotdep = fc_toks[fb - 3] if fb >= 3 else (
                    st[i - 1]["fc_pe"] if i > 0 else None)
                wdep = wfc_wr_tok[fb] if i == 0 else None
                wfc_rd[fb] = DS(lambda e, fb=fb: e.dma_start(
                    out=wfc_buf[:, (fb % 3) * 1024:(fb % 3 + 1) * 1024],
                    in_=wfc_s[fb, :, :]), deps=[slotdep, wdep])

            for fb in range(3):
                fc_prefetch(fb)
            h_toks = [None] * FB
            for fb in range(FB):
                sl, pcols, btok = pslot()
                for cb in range(CB):
                    T(lambda e, fb=fb, cb=cb, p=pcols: e.matmul(
                        p, wfc_buf[:, (fb % 3) * 1024 + cb * 128:
                                   (fb % 3) * 1024 + cb * 128 + 128],
                        xn_ct[:, cb * 512:(cb + 1) * 512],
                        start=(cb == 0), stop=False),
                        deps=([wfc_rd[fb], btok] if cb == 0 else []), inc=False)
                lastf = T(lambda e, fb=fb, p=pcols: e.matmul(
                    p, b_fc_t[:, fb * 128:(fb + 1) * 128],
                    mid_bf[:, :], start=False, stop=True), deps=[mid_tok], inc=True)
                fc_toks[fb] = lastf
                if fb + 3 < FB:
                    fc_prefetch(fb + 3)
                htok = S(lambda e, fb=fb, p=pcols: e.activation(
                    out=h_bf[:, fb * 512:(fb + 1) * 512], in_=p, func=AF.Gelu,
                    scale=scl[:, 0:1], bias=bfc_c[:, fb:fb + 1]), deps=[lastf])
                slot_tok[sl] = htok
                h_toks[fb] = htok
            c["fc_pe"] = lastf
            c["h"] = h_toks

        def emit_mid2_proj(i):
            c = st[i]
            sl, pcols, btok = pslot()
            lastm2 = None
            for fb in range(FB):
                lastm2 = T(lambda e, fb=fb, p=pcols: e.matmul(
                    p[0:16, :], a_pj_t[:, fb * 16:(fb + 1) * 16],
                    h_bf[:, fb * 512:(fb + 1) * 512],
                    start=(fb == 0), stop=(fb == 31)),
                    deps=[c["h"][fb]] + ([btok] if fb == 0 else []), inc=(fb == 31))
            mid2_tok = S(lambda e, p=pcols: e.activation(
                out=mid2_bf[:, :], in_=p[0:16, :], func=AF.Copy,
                scale=scl[0:16, 5:6]), deps=[lastm2])
            slot_tok[sl] = mid2_tok

            sg_toks = [None] * 8
            wpj_rd = [None] * 16
            half_toks = [None] * 16

            def pj_prefetch(ha):
                db_, hh_ = ha // 2, ha % 2
                slotdep = half_toks[ha - 3] if ha >= 3 else (
                    st[i - 1]["proj_pe"] if i > 0 else None)
                wdeps = ([wpj_wr_tok[db_][2 * hh_], wpj_wr_tok[db_][2 * hh_ + 1]]
                         if i == 0 else [])
                wpj_rd[ha] = DS(lambda e, ha=ha, db_=db_, hh_=hh_: e.dma_start(
                    out=wpj_buf[:, (ha % 3) * 2048:(ha % 3 + 1) * 2048],
                    in_=wpj_s[db_, :, hh_ * 2048:(hh_ + 1) * 2048]),
                    deps=[slotdep] + wdeps)

            for ha in range(3):
                pj_prefetch(ha)
            for db in range(8):
                sl, pcols, btok = pslot()
                for fb in range(FB):
                    hh = fb // 16
                    ha = db * 2 + hh
                    fq = fb % 16
                    mm = T(lambda e, fb=fb, ha=ha, fq=fq, p=pcols: e.matmul(
                        p, wpj_buf[:, (ha % 3) * 2048 + fq * 128:
                                   (ha % 3) * 2048 + fq * 128 + 128],
                        h_bf[:, fb * 512:(fb + 1) * 512],
                        start=(fb == 0), stop=False),
                        deps=([wpj_rd[ha], btok] if fq == 0 else []),
                        inc=(fb == 15))
                    if fb == 15:
                        half_toks[db * 2] = mm
                        if db * 2 + 3 < 16:
                            pj_prefetch(db * 2 + 3)
                lastp = T(lambda e, db=db, p=pcols: e.matmul(
                    p, b_pj_t[:, db * 128:(db + 1) * 128],
                    mid2_bf[:, :], start=False, stop=True),
                    deps=[mid2_tok], inc=True)
                half_toks[db * 2 + 1] = lastp
                if db * 2 + 4 < 16:
                    pj_prefetch(db * 2 + 4)
                gtok = S(lambda e, db=db, p=pcols: e.activation(
                    out=g_s[:, (db % 2) * 512:(db % 2) * 512 + 512], in_=p,
                    func=AF.Silu, scale=scl[:, 3:4], bias=bpj_c[:, db:db + 1]),
                    deps=[lastp, sg_toks[db - 2] if db >= 2 else None,
                          c["s_ev"] if db == 0 else None])
                sgt = V(lambda e, db=db: e.tensor_tensor(
                    s_sb[:, db * 512:(db + 1) * 512],
                    s_sb[:, db * 512:(db + 1) * 512],
                    g_s[:, (db % 2) * 512:(db % 2) * 512 + 512], ALU.mult),
                    deps=[gtok])
                slot_tok[sl] = sgt
                sg_toks[db] = sgt
            c["proj_pe"] = lastp
            c["sg"] = sg_toks

        def emit_G(i):
            c = st[i]
            lastt = None
            stores = [None] * 4
            ldr = [None] * 4
            prev_stores = st[i - 1]["stores"] if i > 0 else None
            for tb in range(4):
                src = xd[i * TC + tb * 128: i * TC + (tb + 1) * 128, :]
                ldr[tb] = DA(lambda e, s=src, sl=tb % 2: e.dma_start(
                    out=xres[:, sl * 1024:(sl + 1) * 1024], in_=s),
                    deps=[stores[tb - 2]] if tb >= 2 else
                    ([prev_stores[tb + 2]] if prev_stores else []))
                add_toks = []
                for half in range(2):
                    sl, pcols, btok = pslot()
                    for k4 in range(4):
                        db = half * 4 + k4
                        lastt = T(lambda e, db=db, tb=tb, k4=k4, p=pcols:
                                  e.transpose(
                                      p[:, k4 * 128:(k4 + 1) * 128],
                                      s_sb[:, db * 512 + tb * 128:
                                           db * 512 + tb * 128 + 128],
                                      s_eyef[:, :]),
                                  deps=[c["sg"][db]] + ([btok] if k4 == 0 else []),
                                  inc=(k4 == 3))
                    atok = V(lambda e, tb=tb, half=half, p=pcols, sl2=tb % 2:
                             e.tensor_tensor(
                                 xres[:, sl2 * 1024 + half * 512:
                                      sl2 * 1024 + half * 512 + 512],
                                 p,
                                 xres[:, sl2 * 1024 + half * 512:
                                      sl2 * 1024 + half * 512 + 512],
                                 ALU.add), deps=[lastt, ldr[tb]])
                    slot_tok[sl] = atok
                    add_toks.append(atok)
                dsty = y[i * TC + tb * 128: i * TC + (tb + 1) * 128, :]
                stores[tb] = DA(lambda e, d=dsty, sl=tb % 2: e.dma_start(
                    out=d, in_=xres[:, sl * 1024:(sl + 1) * 1024]),
                    deps=[add_toks[-1]])
            c["sgT_pe"] = lastt
            c["stores"] = stores
            c["store"] = stores[3]

        emit_xld(0)
        emit_LN(0)
        emit_T(0)
        emit_mix_kv(0)
        for i in range(NCH):
            if i + 1 < NCH:
                emit_xld(i + 1)
            ch, dr = (None, None)
            if i > 0:
                ch, dr = emit_wkv(i - 1)
            emit_kv(i, ch, dr)
            if i > 0:
                emit_Wo(i - 1)
            emit_r(i)
            if i > 0:
                emit_mid2_proj(i - 1)
            emit_mid_fc(i)
            if i + 1 < NCH:
                emit_LN(i + 1)
                emit_T(i + 1)
                emit_mix_kv(i + 1)
            if i > 0:
                emit_G(i - 1)
        emit_wkv(NCH - 1, standalone=True)
        emit_Wo(NCH - 1)
        emit_mid2_proj(NCH - 1)
        emit_G(NCH - 1)

        final_store = st[NCH - 1]["store"]


        with nc.Block() as block:
            def replay(handle, name):
                for waits, fn, sem, step in sch.ops[name]:
                    for k, v in waits:
                        handle.wait_ge(semmap[k], v)
                    inst = fn(handle)
                    if sem is not None:
                        inst.then_inc(semmap[sem], step)

            @block.tensor
            def _(e):
                replay(e, "tensor")

            @block.vector
            def _(e):
                replay(e, "vector")

            @block.gpsimd
            def _(e):
                replay(e, "gpsimd")

            @block.scalar
            def _(e):
                replay(e, "scalar")
                for i in range(NQA):
                    if sch.cnt[f"qa{i}"]:
                        e.wait_ge(semmap[f"qa{i}"], sch.cnt[f"qa{i}"])

            @block.sync
            def _(e):
                replay(e, "sync")
                for i in range(NQS):
                    if sch.cnt[f"qs{i}"]:
                        e.wait_ge(semmap[f"qs{i}"], sch.cnt[f"qs{i}"])
                for i in range(NQA):
                    if sch.cnt[f"qa{i}"]:
                        e.wait_ge(semmap[f"qa{i}"], sch.cnt[f"qa{i}"])

    return nc


_NC_CACHE = [None]


def kernel(L0_attn_x, L0_attn_time_decay, L0_attn_time_first, L0_attn_time_mix_k,
           L0_attn_time_mix_v, L0_attn_time_mix_r, L0_attn_Wk, L0_attn_Wv,
           L0_attn_Wr, L0_attn_Wo, L0_ffn_W_fc, L0_ffn_b_fc, L0_ffn_A_fc,
           L0_ffn_B_fc, L0_ffn_W_proj, L0_ffn_b_proj, L0_ffn_A_proj,
           L0_ffn_B_proj):
    x = np.ascontiguousarray(np.asarray(L0_attn_x, dtype=np.float32))
    f32 = lambda a, shape=None: np.ascontiguousarray(
        np.asarray(a, dtype=np.float32).reshape(shape)
        if shape else np.asarray(a, dtype=np.float32))
    common = {
        "w_td": f32(L0_attn_time_decay, (8, 128)),
        "w_tf": f32(L0_attn_time_first, (8, 128)),
        "w_tmk": f32(L0_attn_time_mix_k, (8, 128)),
        "w_tmv": f32(L0_attn_time_mix_v, (8, 128)),
        "w_tmr": f32(L0_attn_time_mix_r, (8, 128)),
        "w_wk": f32(L0_attn_Wk), "w_wv": f32(L0_attn_Wv),
        "w_wr": f32(L0_attn_Wr), "w_wo": f32(L0_attn_Wo),
        "w_wfc": f32(L0_ffn_W_fc), "w_bfc": f32(L0_ffn_b_fc, (32, 128)),
        "w_afc": f32(L0_ffn_A_fc), "w_bfcB": f32(L0_ffn_B_fc),
        "w_wpj": f32(L0_ffn_W_proj), "w_bpj": f32(L0_ffn_b_proj, (8, 128)),
        "w_apj": f32(L0_ffn_A_proj), "w_bpjB": f32(L0_ffn_B_proj),
        "eye_f": np.eye(128, dtype=np.float32),
        "eye_b": np.eye(128).astype(BF),
        "ones1": np.ones((1, 128), dtype=np.float32),
    }
    in_maps = []
    for c in range(NCORE):
        m = dict(common)
        m["xd"] = np.ascontiguousarray(x[2 * c:2 * c + 2].reshape(ROWS, C))
        in_maps.append(m)
    if _NC_CACHE[0] is None:
        _NC_CACHE[0] = build_nc()
    nc = _NC_CACHE[0]
    res = run_bass_kernel_spmd(nc, in_maps, core_ids=list(range(NCORE)))
    out = np.empty((16, 4096, 1024), dtype=np.float32)
    for c in range(NCORE):
        out[2 * c:2 * c + 2] = res.results[c]["y"].reshape(2, 4096, 1024)
    return out


if __name__ == "__main__":
    np.random.seed(0)
    build_nc()
    print("build ok")
